# version 09
# baseline (speedup 1.0000x reference)
import contextlib
import numpy as np
import ml_dtypes
import concourse.bass as bass
import concourse.mybir as mybir
from concourse.bass_utils import run_bass_kernel_spmd

F32 = mybir.dt.float32
BF16 = mybir.dt.bfloat16
I32 = mybir.dt.int32
AF = mybir.ActivationFunctionType
AOP = mybir.AluOpType
AX = mybir.AxisListType

D = 2048
T = 8192
B = 2
TC = 2048
NH = 16
HPC = 4
DH = 128
I8 = 8192
NCORES = 8
GS = 4
SCL = 0.08838834764831845
ISQ_D = 0.022097086912079608
ISQ_I = 0.011048543456039804
TBLK = 512
NTB = TC // TBLK

DEBUG = False
STAGES = 99


class Ctr:
    def __init__(self, h):
        self.h = h
        self.n = 0

    def inc(self, k=1):
        self.n += k
        return self.n


class Stream:

    def __init__(self, sems, gsize=1):
        self.sems = sems
        self.k = len(sems)
        self.gsize = gsize
        self.bases = [s.n for s in sems]
        self.gi = 0
        self.di = 0

    def issue(self, instr):
        s = self.sems[self.gi % self.k]
        instr.then_inc(s.h, 16)
        s.inc(16)
        self.di += 1
        if self.di == self.gsize:
            self.di = 0
            self.gi += 1

    def tok(self, gidx):
        assert 0 <= gidx, gidx
        s = self.sems[gidx % self.k]
        return (s.h, self.bases[gidx % self.k] + 16 * self.gsize * (gidx // self.k + 1))

    def final(self):
        assert self.di == 0
        return self.tok(self.gi - 1) if self.gi > 0 else None


def ap2(t, r0, nr, c0, ncol):
    C = t.shape[1]
    return bass.AP(t, r0 * C + c0, [[C, nr], [1, ncol]])


def ap3(t, r0, c0, nr, cstep, nch, inner):
    C = t.shape[1]
    return bass.AP(t, r0 * C + c0, [[C, nr], [cstep, nch], [1, inner]])


def sb3(t, c0, cstep, nch, inner, np_=128):
    F = t.shape[1]
    return bass.AP(t, c0, [[F, np_], [cstep, nch], [1, inner]])


def build(debug=False, stages=99):
    nc = bass.Bass("TRN2", target_bir_lowering=False)

    x_fm = nc.dram_tensor("x_fm", [D, TC], F32, kind="ExternalInput")
    wqT = nc.dram_tensor("wqT", [D, HPC * DH], F32, kind="ExternalInput")
    wkT = nc.dram_tensor("wkT", [D, HPC * DH], F32, kind="ExternalInput")
    wvT = nc.dram_tensor("wvT", [D, HPC * DH], F32, kind="ExternalInput")
    woT = nc.dram_tensor("woT", [D, D], F32, kind="ExternalInput")
    wup_r = nc.dram_tensor("wup_r", [I8 // NCORES, D], F32, kind="ExternalInput")
    wgt_r = nc.dram_tensor("wgt_r", [I8 // NCORES, D], F32, kind="ExternalInput")
    wdn_r = nc.dram_tensor("wdn_r", [D // NCORES, I8], F32, kind="ExternalInput")
    su_ug = nc.dram_tensor("su_ug", [128, 16], F32, kind="ExternalInput")
    su_dp = nc.dram_tensor("su_dp", [128, 64], F32, kind="ExternalInput")
    wsc_up = nc.dram_tensor("wsc_up", [128, 64], F32, kind="ExternalInput")
    wsc_gt = nc.dram_tensor("wsc_gt", [128, 64], F32, kind="ExternalInput")
    wsc_dp = nc.dram_tensor("wsc_dp", [128, 16], F32, kind="ExternalInput")
    ident = nc.dram_tensor("ident", [128, 128], F32, kind="ExternalInput")
    maskm = nc.dram_tensor("maskm", [128, 896], F32, kind="ExternalInput")
    sel16 = nc.dram_tensor("sel16", [16, 16 * 128], F32, kind="ExternalInput")
    tok0 = nc.dram_tensor("tok0", [1, 4], I32, kind="ExternalInput")

    out_fm = nc.dram_tensor("out_fm", [D, TC], F32, kind="ExternalOutput")

    e1in_h = [nc.dram_tensor(f"e1in{h}", [D, TC // 2], BF16) for h in range(2)]
    e1out_h = [nc.dram_tensor(f"e1out{h}", [NCORES * D, TC // 2], BF16) for h in range(2)]
    qs = nc.dram_tensor("qs", [HPC * DH, T], BF16)
    ks = nc.dram_tensor("ks", [HPC * DH, T], BF16)
    vs = [nc.dram_tensor(f"vs{h}", [T, DH], BF16) for h in range(HPC)]
    e2in_h = [nc.dram_tensor(f"e2in{h}", [DH, T], BF16) for h in range(HPC)]
    e2out_h = [nc.dram_tensor(f"e2out{h}", [NCORES * DH, T], BF16) for h in range(HPC)]
    e2sin = nc.dram_tensor("e2sin", [HPC, T], F32)
    e2sout_h = [nc.dram_tensor(f"e2sout{h}", [NCORES, T], F32) for h in range(HPC)]
    e3u_in = nc.dram_tensor("e3u_in", [D, 2048], BF16)
    e3u_out = nc.dram_tensor("e3u_out", [NCORES * D, 2048], BF16)
    e3d_in = nc.dram_tensor("e3d_in", [I8, 256], BF16)
    e3d_out = nc.dram_tensor("e3d_out", [NCORES * I8, 256], BF16)
    x1sp = nc.dram_tensor("x1sp", [D, TC], F32)
    u_dram = nc.dram_tensor("u_dram", [D, TC], BF16)

    dbg = {}
    if debug:
        for name, shp, dt in [
            ("d_e1", [NCORES * D, TC // 2], BF16), ("d_q", [HPC * DH, T], BF16),
            ("d_k", [HPC * DH, T], BF16), ("d_v0", [T, DH], BF16),
            ("d_e2", [NCORES * DH, T], BF16), ("d_e2s", [HPC, T], F32),
            ("d_x1", [D, TC], F32), ("d_u", [D, TC], BF16),
            ("d_e3u", [NCORES * D, 2048], BF16), ("d_e3d", [NCORES * I8, 256], BF16),
        ]:
            dbg[name] = nc.dram_tensor(name, shp, dt, kind="ExternalOutput")

    st = contextlib.ExitStack()
    NSY, NGP = 12, 6
    SY = [Ctr(st.enter_context(nc.semaphore(f"sy{i}"))) for i in range(NSY)]
    GP = [Ctr(st.enter_context(nc.semaphore(f"gp{i}"))) for i in range(NGP)]
    mm = Ctr(st.enter_context(nc.semaphore("mm")))
    sv = Ctr(st.enter_context(nc.semaphore("sv")))
    sa = Ctr(st.enter_context(nc.semaphore("sa")))
    cc = Ctr(st.enter_context(nc.semaphore("cc")))
    cc2 = Ctr(st.enter_context(nc.semaphore("cc2")))
    cc3 = Ctr(st.enter_context(nc.semaphore("cc3")))
    g2 = Ctr(st.enter_context(nc.semaphore("g2")))
    hcc = [Ctr(st.enter_context(nc.semaphore(f"hcc{h}"))) for h in range(HPC)]
    all_streams = []

    def mk(sems, gsize=1):
        s = Stream(sems, gsize)
        all_streams.append(s)
        return s

    ones_bf = st.enter_context(nc.sbuf_tensor("ones_bf", [128, 1], BF16))
    ones128 = st.enter_context(nc.sbuf_tensor("ones128", [128, 128], BF16))
    ones_f1 = st.enter_context(nc.sbuf_tensor("ones_f1", [1, 128], F32))
    ones_fc = st.enter_context(nc.sbuf_tensor("ones_fc", [128, 1], F32))
    ident_sb = st.enter_context(nc.sbuf_tensor("ident_sb", [128, 128], F32))
    mask_sb = st.enter_context(nc.sbuf_tensor("mask_sb", [128, 896], F32))
    sel_sb = st.enter_context(nc.sbuf_tensor("sel_sb", [16, 16 * 128], F32))
    suu_sb = st.enter_context(nc.sbuf_tensor("suu_sb", [128, 16], F32))
    sud_sb = st.enter_context(nc.sbuf_tensor("sud_sb", [128, 64], F32))
    wsu_sb = st.enter_context(nc.sbuf_tensor("wsu_sb", [128, 64], F32))
    wsg_sb = st.enter_context(nc.sbuf_tensor("wsg_sb", [128, 64], F32))
    wsd_sb = st.enter_context(nc.sbuf_tensor("wsd_sb", [128, 16], F32))
    tok_sb = st.enter_context(nc.sbuf_tensor("tok_sb", [1, 4], I32))
    bquad = st.enter_context(nc.sbuf_tensor("bquad", [1, 64], F32))
    brow = st.enter_context(nc.sbuf_tensor("brow", [1, 16], F32))
    bias4 = st.enter_context(nc.sbuf_tensor("bias4", [128, 4], F32))
    eps_sb = st.enter_context(nc.sbuf_tensor("eps_sb", [1, 1], F32))

    def _finish(upto):
        if debug:
            allpairs = [("d_e1", e1out_h[0], 1), ("d_q", qs, 2), ("d_k", ks, 2), ("d_v0", vs[0], 2),
                        ("d_e2", e2out_h[0], 3), ("d_e2s", e2sin, 3), ("d_x1", x1sp, 4), ("d_u", u_dram, 5),
                        ("d_e3u", e3u_out, 3), ("d_e3d", e3d_out, 3)]
            pairs = [(n_, s_) for n_, s_, k_ in allpairs if k_ <= upto]
            finals = [s_.final() for s_ in all_streams]
            stD = mk([SY[0]], gsize=len(pairs))
            with nc.Block() as blk:
                @blk.sync
                def _(sy):
                    for f in finals:
                        if f is not None:
                            sy.wait_ge(*f)
                    sy.wait_ge(cc.h, cc.n)
                    sy.wait_ge(cc2.h, cc2.n)
                    sy.wait_ge(cc3.h, cc3.n)
                    for h_ in range(HPC):
                        sy.wait_ge(hcc[h_].h, hcc[h_].n)
                    sy.wait_ge(sv.h, sv.n)
                    sy.wait_ge(sa.h, sa.n)
                    sy.wait_ge(mm.h, mm.n)
                    for name, src_ in pairs:
                        stD.issue(sy.dma_start(out=dbg[name].ap(), in_=src_.ap()))
                    f = stD.final()
                    if f is not None:
                        sy.wait_ge(*f)
        st.close()
        return nc

    raw_d = nc.dram_tensor("raw_d", [256, I8], F32)

    V0, MM0, A0, G0 = sv.n, mm.n, sa.n, g2.n
    stC = mk([SY[0]], gsize=9)
    stWd = mk([SY[1]])
    stRW = mk([SY[6]])
    stL = mk([SY[2], SY[3]])
    stS = mk([SY[4], SY[5]])
    WSC_V = V0 + 1
    DBF = [V0 + 35 + (i + 1) for i in range(2)]
    SQ = [A0 + i + 1 for i in range(16)]
    SQT = A0 + 17
    MMD = [MM0 + i + 1 for i in range(16)]
    BC = MM0 + 17
    V_COMB = V0 + 2
    RECIP = V0 + 3
    NRM = {}
    for _h in range(2):
        for _i in range(16):
            NRM[(_h, _i)] = V0 + 3 + 16 * _h + _i + 1

    with nc.Block() as blk, contextlib.ExitStack() as ph:
        bfdA = ph.enter_context(nc.sbuf_tensor("bfdA", [128, I8], F32))
        bfdB = ph.enter_context(nc.sbuf_tensor("bfdB", [128, I8], F32))
        xc = [ph.enter_context(nc.sbuf_tensor(f"ln1_xc{p}", [128, TC], F32)) for p in range(2)]
        sq = ph.enter_context(nc.sbuf_tensor("ln1_sq", [128, TC], F32))
        xn = [ph.enter_context(nc.sbuf_tensor(f"ln1_xn{p}", [128, TC], BF16)) for p in range(2)]
        t1 = ph.enter_context(nc.sbuf_tensor("ln1_t1", [128, TC], F32))
        s_m = ph.enter_context(nc.sbuf_tensor("ln1_sm", [1, TC], F32))
        s_r = ph.enter_context(nc.sbuf_tensor("ln1_sr", [1, TC], F32))
        s_t = ph.enter_context(nc.sbuf_tensor("ln1_st", [1, TC], F32))
        psm = ph.enter_context(nc.psum_tensor("ln1_psm", [128, TC], F32))
        psr = ph.enter_context(nc.psum_tensor("ln1_psr", [128, TC], F32))

        @blk.sync
        def _(sy):
            for tsrc, tdst in [(ident, ident_sb), (maskm, mask_sb), (sel16, sel_sb),
                               (su_ug, suu_sb), (su_dp, sud_sb), (wsc_up, wsu_sb),
                               (wsc_gt, wsg_sb), (wsc_dp, wsd_sb), (tok0, tok_sb)]:
                stC.issue(sy.dma_start(out=tdst[:, :], in_=tsrc[:, :]))
            stWd.issue(sy.dma_start(out=bfdA[:, :], in_=ap2(wdn_r, 0, 128, 0, I8)))
            for i in range(16):
                if i >= 2:
                    sy.wait_ge(mm.h, MMD[i - 2])
                stL.issue(sy.dma_start(out=xc[i % 2][:, :], in_=ap2(x_fm, i * 128, 128, 0, TC)))
            for h in range(2):
                for i in range(18):
                    if i < 16:
                        if 16 * h + i >= 2:
                            ph_, pi_ = divmod(16 * h + i - 2, 16)
                            sy.wait_ge(sv.h, NRM[(ph_, pi_)])
                        else:
                            sy.wait_ge(mm.h, MMD[15])
                        stL.issue(sy.dma_start(out=xc[i % 2][:, 0:TC // 2],
                                               in_=ap2(x_fm, i * 128, 128, h * (TC // 2), TC // 2)))
                    if i >= 2:
                        sy.wait_ge(sv.h, NRM[(h, i - 2)])
                        stS.issue(sy.dma_start(out=ap2(e1in_h[h], (i - 2) * 128, 128, 0, TC // 2),
                                               in_=xn[(i - 2) % 2][:, 0:TC // 2]))
            sy.wait_ge(sv.h, DBF[0])
            stRW.issue(sy.dma_start(out=ap2(raw_d, 0, 128, 0, I8), in_=bfdB[:, :]))
            stWd.issue(sy.dma_start(out=bfdA[:, :], in_=ap2(wdn_r, 128, 128, 0, I8)))
            sy.wait_ge(sv.h, DBF[1])
            stRW.issue(sy.dma_start(out=ap2(raw_d, 128, 128, 0, I8), in_=bfdB[:, :]))

        @blk.gpsimd
        def _(g):
            g.memset(ones_bf[:, :], 1.0)
            g.memset(ones128[:, :], 1.0)
            g.memset(ones_fc[:, :], 1.0)
            g.memset(eps_sb[:, :], 1e-5)
            g.memset(ones_f1[:, :], 1.0).then_inc(g2.h, 1)
            g2.inc()
            g.wait_ge(*stS.tok(15))
            g.collective_compute("AllGather", AOP.bypass,
                                 replica_groups=[list(range(NCORES))],
                                 ins=[e1in_h[0].ap().opt()],
                                 outs=[e1out_h[0].ap().opt()]).then_inc(cc.h, 1)
            cc.inc()
            g.wait_ge(*stS.tok(31))
            g.collective_compute("AllGather", AOP.bypass,
                                 replica_groups=[list(range(NCORES))],
                                 ins=[e1in_h[1].ap().opt()],
                                 outs=[e1out_h[1].ap().opt()]).then_inc(cc3.h, 1)
            cc3.inc()

        @blk.scalar
        def _(s):
            for i in range(16):
                s.wait_ge(*stL.tok(i))
                if i >= 1:
                    s.wait_ge(mm.h, MMD[i - 1])
                s.square(sq[:, :], xc[i % 2][:, :]).then_inc(sa.h, 1)
                assert sa.inc() == SQ[i]
            s.wait_ge(sv.h, V_COMB)
            s.wait_ge(g2.h, G0 + 1)
            s.activation(s_t[:1, :], s_t[:1, :], AF.Sqrt, bias=eps_sb[:1, :1]).then_inc(sa.h, 1)
            assert sa.inc() == SQT

        @blk.tensor
        def _(t):
            t.wait_ge(g2.h, G0 + 1)
            for i in range(16):
                t.wait_ge(sa.h, SQ[i])
                last = None
                for c4 in range(4):
                    t.matmul(psm[0:1, c4 * 512:(c4 + 1) * 512], ones_fc[:, :1],
                             xc[i % 2][:, c4 * 512:(c4 + 1) * 512],
                             start=(i == 0), stop=(i == 15))
                    last = t.matmul(psr[0:1, c4 * 512:(c4 + 1) * 512], ones_fc[:, :1],
                                    sq[:, c4 * 512:(c4 + 1) * 512],
                                    start=(i == 0), stop=(i == 15))
                last.then_inc(mm.h, 1)
                assert mm.inc() == MMD[i]
            t.wait_ge(sv.h, RECIP)
            last = None
            for c4 in range(4):
                t.matmul(psm[:, c4 * 512:(c4 + 1) * 512], ones_f1[:1, :],
                         s_m[:1, c4 * 512:(c4 + 1) * 512], start=True, stop=True)
                last = t.matmul(psr[:, c4 * 512:(c4 + 1) * 512], ones_f1[:1, :],
                                s_r[:1, c4 * 512:(c4 + 1) * 512], start=True, stop=True)
            last.then_inc(mm.h, 1)
            assert mm.inc() == BC

        @blk.vector
        def _(v):
            v.wait_ge(*stC.tok(0))
            v.tensor_scalar(wsu_sb[:, :], wsu_sb[:, :], ISQ_D, None, AOP.mult)
            v.tensor_scalar(wsg_sb[:, :], wsg_sb[:, :], ISQ_D, None, AOP.mult)
            v.tensor_scalar(wsd_sb[:, :], wsd_sb[:, :], ISQ_I, None, AOP.mult).then_inc(sv.h, 1)
            assert sv.inc() == WSC_V
            v.wait_ge(mm.h, MMD[15])
            v.tensor_scalar(s_m[:1, :], psm[0:1, :], 1.0 / D, None, AOP.mult)
            v.tensor_scalar(s_t[:1, :], psr[0:1, :], 1.0 / D, None, AOP.mult)
            v.tensor_tensor(s_r[:1, :], s_m[:1, :], s_m[:1, :], AOP.mult)
            v.tensor_tensor(s_t[:1, :], s_t[:1, :], s_r[:1, :], AOP.subtract).then_inc(sv.h, 1)
            assert sv.inc() == V_COMB
            v.wait_ge(sa.h, SQT)
            v.reciprocal(s_r[:1, :], s_t[:1, :]).then_inc(sv.h, 1)
            assert sv.inc() == RECIP
            for h in range(2):
                for i in range(16):
                    v.wait_ge(*stL.tok(16 + 16 * h + i))
                    v.wait_ge(mm.h, BC)
                    if 16 * h + i >= 2:
                        v.wait_ge(*stS.tok(16 * h + i - 2))
                    hoff = h * (TC // 2)
                    v.tensor_tensor(t1[:, 0:TC // 2], xc[i % 2][:, 0:TC // 2],
                                    psm[:, hoff:hoff + TC // 2], AOP.subtract)
                    v.tensor_tensor(xn[i % 2][:, 0:TC // 2], t1[:, 0:TC // 2],
                                    psr[:, hoff:hoff + TC // 2], AOP.mult).then_inc(sv.h, 1)
                    assert sv.inc() == NRM[(h, i)]
            for i in range(2):
                v.wait_ge(*stWd.tok(i))
                if i >= 1:
                    v.wait_ge(*stRW.tok(i - 1))
                sdb, ddb = bfdA, bfdB
                h = 1
                last = None
                while h < I8:
                    nb = I8 // (2 * h)
                    a_ = bass.AP(sdb, 0, [[I8, 128], [2 * h, nb], [1, h]])
                    b_ = bass.AP(sdb, h, [[I8, 128], [2 * h, nb], [1, h]])
                    oa = bass.AP(ddb, 0, [[I8, 128], [2 * h, nb], [1, h]])
                    ob = bass.AP(ddb, h, [[I8, 128], [2 * h, nb], [1, h]])
                    v.tensor_tensor(oa, a_, b_, AOP.add)
                    last = v.tensor_tensor(ob, a_, b_, AOP.subtract)
                    sdb, ddb = ddb, sdb
                    h *= 2
                assert sdb is bfdB
                last.then_inc(sv.h, 1)
                assert sv.inc() == DBF[i]

    E1_CC = cc.n
    ln1_stS = stS
    if stages <= 1:
        return _finish(1)

    V0, MM0, A0 = sv.n, mm.n, sa.n
    NW = 8
    stWin = mk([GP[2], GP[3]])
    stQs = mk([SY[2], SY[3]])
    stKs = mk([SY[4], SY[5]])
    stVs = mk([SY[7], SY[8]], gsize=4)
    stWq = mk([GP[1]], gsize=3)
    MM_W = [MM0 + 24 * (w + 1) for w in range(NW)]
    QMM = lambda w, oq: MM0 + 24 * w + oq + 1
    KMM = lambda w, oq: MM0 + 24 * w + 4 + oq + 1
    VMM = lambda w, ts: MM0 + 24 * w + 8 + ts + 1
    BQMM = lambda w, oq: MM0 + 24 * w + 16 + oq + 1
    BKMM = lambda w, oq: MM0 + 24 * w + 20 + oq + 1
    QEV = lambda w, oq: A0 + 24 * w + 2 * oq + 1
    QSQ = lambda w, oq: A0 + 24 * w + 2 * oq + 2
    KEV = lambda w, oq: A0 + 24 * w + 8 + 2 * oq + 1
    KSQ = lambda w, oq: A0 + 24 * w + 8 + 2 * oq + 2
    VEV = lambda w, ts: A0 + 24 * w + 16 + ts + 1
    RBQ = lambda w, oq: V0 + 8 * w + oq + 1
    RBK = lambda w, oq: V0 + 8 * w + 4 + oq + 1

    with nc.Block() as blk, contextlib.ExitStack() as ph:
        win = [ph.enter_context(nc.sbuf_tensor(f"win{p}", [128, 16 * 1024], BF16)) for p in range(2)]
        wq_sb = ph.enter_context(nc.sbuf_tensor("wq_sb", [128, 16 * 512], BF16))
        wk_sb = ph.enter_context(nc.sbuf_tensor("wk_sb", [128, 16 * 512], BF16))
        wv_sb = ph.enter_context(nc.sbuf_tensor("wv_sb", [128, 16 * 512], BF16))
        qe = [ph.enter_context(nc.sbuf_tensor(f"qe{p}", [128, 1024], BF16)) for p in range(2)]
        ke = [ph.enter_context(nc.sbuf_tensor(f"ke{p}", [128, 1024], BF16)) for p in range(2)]
        ve = [ph.enter_context(nc.sbuf_tensor(f"ve{p}", [128, 512], BF16)) for p in range(2)]
        bsq = ph.enter_context(nc.sbuf_tensor("bsq", [128, 8 * 1024], BF16))
        ps_q = ph.enter_context(nc.psum_tensor("ps_q", [128, 1024], F32))
        ps_k = ph.enter_context(nc.psum_tensor("ps_k", [128, 1024], F32))
        ps_v = ph.enter_context(nc.psum_tensor("ps_v", [128, 512], F32))
        ps_b = ph.enter_context(nc.psum_tensor("ps_b", [1, 1024], F32))

        @blk.gpsimd
        def _(g):
            for wsrc, wdst in [(wqT, wq_sb), (wkT, wk_sb), (wvT, wv_sb)]:
                stWq.issue(g.dma_start(out=wdst[:, :], in_=ap3(wsrc, 0, 0, 128, 128 * 512, 16, 512)))
            g.wait_ge(cc.h, E1_CC)
            with g.register("rw0") as rw0, g.register("rw1") as rw1:
                g.reg_load(rw0, tok_sb[:1, 1:2])
                for w in range(NW):
                    rk, hhalf = w // 2, w % 2
                    if hhalf == 1:
                        g.wait_ge(cc3.h, 1)
                    if w >= 2:
                        g.wait_ge(mm.h, MM_W[w - 2])
                    g.reg_add(rw1, rw0, rk * D * (TC // 2))
                    stWin.issue(g.dma_start(
                        out=win[w % 2][:, :],
                        in_=bass.AP(e1out_h[hhalf], rw1,
                                    [[TC // 2, 128], [128 * (TC // 2), 16], [1, 1024]])))

        @blk.sync
        def _(sy):
            for w in range(NW):
                for oq in range(4):
                    sy.wait_ge(sa.h, QEV(w, oq))
                    stQs.issue(sy.dma_start(out=ap2(qs, oq * 128, 128, w * 1024, 1024), in_=qe[oq % 2][:, :]))
                for oq in range(4):
                    sy.wait_ge(sa.h, KEV(w, oq))
                    stKs.issue(sy.dma_start(out=ap2(ks, oq * 128, 128, w * 1024, 1024), in_=ke[oq % 2][:, :]))
                for ts in range(8):
                    sy.wait_ge(sa.h, VEV(w, ts))
                    for hh in range(4):
                        stVs.issue(sy.dma_start(out=ap2(vs[hh], w * 1024 + ts * 128, 128, 0, 128),
                                                in_=ve[ts % 2][:, hh * 128:(hh + 1) * 128]))

        @blk.tensor
        def _(t):
            t.wait_ge(*stWq.tok(0))
            for w in range(NW):
                t.wait_ge(*stWin.tok(w))
                for oq in range(4):
                    if w > 0 or oq >= 1:
                        t.wait_ge(sa.h, QEV(w, oq - 1) if oq >= 1 else QEV(w - 1, 3))
                    last = None
                    for i in range(16):
                        for c2 in range(2):
                            last = t.matmul(ps_q[:, c2 * 512:(c2 + 1) * 512],
                                            wq_sb[:, i * 512 + oq * 128: i * 512 + oq * 128 + 128],
                                            win[w % 2][:, i * 1024 + c2 * 512: i * 1024 + (c2 + 1) * 512],
                                            start=(i == 0), stop=(i == 15))
                    last.then_inc(mm.h, 1)
                    assert mm.inc() == QMM(w, oq)
                for oq in range(4):
                    if w > 0 or oq >= 1:
                        t.wait_ge(sa.h, KEV(w, oq - 1) if oq >= 1 else KEV(w - 1, 3))
                    last = None
                    for i in range(16):
                        for c2 in range(2):
                            last = t.matmul(ps_k[:, c2 * 512:(c2 + 1) * 512],
                                            wk_sb[:, i * 512 + oq * 128: i * 512 + oq * 128 + 128],
                                            win[w % 2][:, i * 1024 + c2 * 512: i * 1024 + (c2 + 1) * 512],
                                            start=(i == 0), stop=(i == 15))
                    last.then_inc(mm.h, 1)
                    assert mm.inc() == KMM(w, oq)
                for ts in range(8):
                    if w > 0 or ts >= 1:
                        t.wait_ge(sa.h, VEV(w, ts - 1) if ts >= 1 else VEV(w - 1, 7))
                    last = None
                    for i in range(16):
                        last = t.matmul(ps_v[:, :],
                                        win[w % 2][:, i * 1024 + ts * 128: i * 1024 + ts * 128 + 128],
                                        wv_sb[:, i * 512:(i + 1) * 512],
                                        start=(i == 0), stop=(i == 15))
                    last.then_inc(mm.h, 1)
                    assert mm.inc() == VMM(w, ts)
                for oq in range(4):
                    t.wait_ge(sa.h, QSQ(w, oq))
                    if w > 0 or oq >= 1:
                        t.wait_ge(sv.h, RBK(w - 1, 3) if oq == 0 else RBQ(w, oq - 1))
                    c0 = oq * 1024
                    t.matmul(ps_b[:1, 0:512], ones_bf[:, :], bsq[:, c0:c0 + 512], start=True, stop=True)
                    t.matmul(ps_b[:1, 512:1024], ones_bf[:, :], bsq[:, c0 + 512:c0 + 1024],
                             start=True, stop=True).then_inc(mm.h, 1)
                    assert mm.inc() == BQMM(w, oq)
                for oq in range(4):
                    t.wait_ge(sa.h, KSQ(w, oq))
                    t.wait_ge(sv.h, RBQ(w, 3) if oq == 0 else RBK(w, oq - 1))
                    c0 = (4 + oq) * 1024
                    t.matmul(ps_b[:1, 0:512], ones_bf[:, :], bsq[:, c0:c0 + 512], start=True, stop=True)
                    t.matmul(ps_b[:1, 512:1024], ones_bf[:, :], bsq[:, c0 + 512:c0 + 1024],
                             start=True, stop=True).then_inc(mm.h, 1)
                    assert mm.inc() == BKMM(w, oq)

        @blk.scalar
        def _(s):
            lastq = {0: None, 1: None}
            lastk = {0: None, 1: None}
            lastv = {0: None, 1: None}
            for w in range(NW):
                for oq in range(4):
                    s.wait_ge(mm.h, QMM(w, oq))
                    if lastq[oq % 2] is not None:
                        s.wait_ge(*lastq[oq % 2])
                    s.copy(qe[oq % 2][:, :], ps_q[:, :]).then_inc(sa.h, 1)
                    assert sa.inc() == QEV(w, oq)
                    s.square(bsq[:, oq * 1024:(oq + 1) * 1024], qe[oq % 2][:, :]).then_inc(sa.h, 1)
                    assert sa.inc() == QSQ(w, oq)
                    lastq[oq % 2] = stQs.tok(4 * w + oq)
                for oq in range(4):
                    s.wait_ge(mm.h, KMM(w, oq))
                    if lastk[oq % 2] is not None:
                        s.wait_ge(*lastk[oq % 2])
                    s.copy(ke[oq % 2][:, :], ps_k[:, :]).then_inc(sa.h, 1)
                    assert sa.inc() == KEV(w, oq)
                    s.square(bsq[:, (4 + oq) * 1024:(5 + oq) * 1024], ke[oq % 2][:, :]).then_inc(sa.h, 1)
                    assert sa.inc() == KSQ(w, oq)
                    lastk[oq % 2] = stKs.tok(4 * w + oq)
                for ts in range(8):
                    s.wait_ge(mm.h, VMM(w, ts))
                    if lastv[ts % 2] is not None:
                        s.wait_ge(*lastv[ts % 2])
                    s.copy(ve[ts % 2][:, :], ps_v[:, :]).then_inc(sa.h, 1)
                    assert sa.inc() == VEV(w, ts)
                    lastv[ts % 2] = stVs.tok(8 * w + ts)

        @blk.vector
        def _(v):
            for w in range(NW):
                for oq in range(4):
                    v.wait_ge(mm.h, BQMM(w, oq))
                    v.reduce_max(bquad[:1, oq * 8 + w: oq * 8 + w + 1], ps_b[:1, :],
                                 axis=AX.X).then_inc(sv.h, 1)
                    assert sv.inc() == RBQ(w, oq)
                for oq in range(4):
                    v.wait_ge(mm.h, BKMM(w, oq))
                    v.reduce_max(bquad[:1, 32 + oq * 8 + w: 32 + oq * 8 + w + 1], ps_b[:1, :],
                                 axis=AX.X).then_inc(sv.h, 1)
                    assert sv.inc() == RBK(w, oq)

    if stages <= 2:
        return _finish(2)

    V0, MM0, A0, G0 = sv.n, mm.n, sa.n, g2.n
    stH = mk([SY[9], SY[10]], gsize=3)
    stE2 = mk([SY[11], SY[6]], gsize=2)
    stTFv = mk([SY[0]])
    stTFg = mk([SY[1]])
    stWrU = mk([SY[2], SY[3]])
    stDL = mk([SY[4], SY[5]])
    stWrD = mk([SY[7], SY[8]])

    PRO_G = G0 + 1
    PRO_A = A0 + 1
    PRO_MM = MM0 + 4
    sched = []
    for hh in range(HPC):
        for qm in range(16):
            nk = 4 * qm + 4
            for kc in range(nk):
                sched.append((hh, qm, kc, kc == nk - 1, kc - 4 * qm if kc >= 4 * qm else None))
    NSC = len(sched)
    NPAIR = NSC // 2

    VCH = [0, 2, 4, 6, 8, 10, 12, 14]
    GCH = [1, 3, 5, 7, 9, 11, 13, 15]
    PIPE = {k: ("v" if k in VCH else "g") for k in range(16)}
    PIDX = {}
    for i, k in enumerate(VCH):
        PIDX[k] = i
    for i, k in enumerate(GCH):
        PIDX[k] = i
    slot_ld = {k: 2 * k + 14 for k in range(16)}
    slot_bfv = {k: 2 * k + 16 for k in VCH}
    slot_tp = {("u", k): 2 * k + 17 for k in range(16)}
    for g_ in range(8):
        slot_ld[("d", g_)] = 42 + 2 * g_
        slot_tp[("d", g_)] = 44 + 2 * g_
    slot_wr = {k: 2 * k + 18 for k in range(16)}
    slot_wr_d = {g_: 45 + 2 * g_ for g_ in range(8)}
    tp_groups = sorted(
        [("u", k, slot_tp[("u", k)]) for k in range(16)]
        + [("d", g_, slot_tp[("d", g_)]) for g_ in range(8)],
        key=lambda x: (x[2], 0 if x[0] == "u" else 1, x[1]))
    tp_by_slot = {}
    for kind_, id_, s_ in tp_groups:
        tp_by_slot.setdefault(s_, []).append((kind_, id_))
    GBF = {k: PRO_G + PIDX[k] + 1 for k in GCH}

    MSKt, EPIt, BFVt = {}, {}, {}
    cv = V0
    cv += 8
    PRO_V = cv
    cv += 1
    PRO_V2 = cv
    cv += 1
    PRO_V3 = cv
    for n, (hh, qm, kc, last_kc, dm) in enumerate(sched):
        gqm = hh * 16 + qm
        if dm is not None:
            cv += 1
            MSKt[n] = cv
        if last_kc:
            cv += 2
            EPIt[gqm] = cv
            for k in VCH:
                if slot_bfv[k] == gqm:
                    cv += 1
                    BFVt[k] = cv

    EXPPt, AEVq, AEV_LAST = {}, [], {}
    ca = A0 + 1
    for p in range(NPAIR):
        ca += 1
        EXPPt[p] = ca
        n1 = 2 * p + 1
        hh, qm, kc1, last1, _ = sched[n1]
        gqm = hh * 16 + qm
        if last1:
            for kind_, id_ in tp_by_slot.get(gqm, []):
                for j in range(16):
                    ca += 1
                    AEVq.append(ca)
                AEV_LAST[(kind_, id_)] = ca
    PREV_U = {}
    for lst in (VCH, GCH):
        for i, k in enumerate(lst):
            if i >= 2:
                PREV_U[k] = lst[i - 2]

    TPdone, SMMd, SUMMd, ATMMd = {}, {}, {}, {}
    TPtile = []
    cm = MM0 + 4
    _sc = [cm]

    def _s_count(m):
        _sc[0] += 1
        SMMd[m] = _sc[0]

    _s_count(0)
    _s_count(1)
    for n, (hh, qm, kc, last_kc, dm) in enumerate(sched):
        if n % 2 == 0:
            if n + 2 < NSC:
                _s_count(n + 2)
            if n + 3 < NSC:
                _s_count(n + 3)
        _sc[0] += 1
        _sc[0] += 1
        SUMMd[n] = _sc[0]
        if last_kc:
            gqm = hh * 16 + qm
            if qm == 15:
                ATMMd[hh] = _sc[0]
            for kind_, id_ in tp_by_slot.get(gqm, []):
                for _j in range(16):
                    _sc[0] += 1
                    TPtile.append(_sc[0])
                TPdone[(kind_, id_)] = _sc[0]

    with nc.Block() as blk, contextlib.ExitStack() as ph:
        qb = [ph.enter_context(nc.sbuf_tensor(f"qb{p}", [128, T], BF16)) for p in range(2)]
        kb = [ph.enter_context(nc.sbuf_tensor(f"kb{p}", [128, T], BF16)) for p in range(2)]
        vb = [ph.enter_context(nc.sbuf_tensor(f"vb{p}", [128, T], BF16)) for p in range(2)]
        pt = [ph.enter_context(nc.sbuf_tensor(f"pt{p}", [128, 1024], BF16)) for p in range(3)]
        osb = [ph.enter_context(nc.sbuf_tensor(f"osb{p}", [128, 512], BF16)) for p in range(2)]
        ssb = [ph.enter_context(nc.sbuf_tensor(f"ssb{p}", [1, 512], F32)) for p in range(2)]
        bfvA = ph.enter_context(nc.sbuf_tensor("bfvA", [128, D], F32))
        bfvB = ph.enter_context(nc.sbuf_tensor("bfvB", [128, D], F32))
        bfgA = ph.enter_context(nc.sbuf_tensor("bfgA", [128, D], F32))
        bfgB = ph.enter_context(nc.sbuf_tensor("bfgB", [128, D], F32))
        braw = [ph.enter_context(nc.sbuf_tensor(f"braw{p}", [128, D], F32)) for p in range(2)]
        tpe_v = ph.enter_context(nc.sbuf_tensor("tpe_v", [128, 2 * D], BF16))
        tpe_g = ph.enter_context(nc.sbuf_tensor("tpe_g", [128, 2 * D], BF16))
        tpe_d = ph.enter_context(nc.sbuf_tensor("tpe_d", [128, 2 * D], BF16))
        ps_s4 = ph.enter_context(nc.psum_tensor("ps_s4", [128, 2048], F32))
        ps_acc = ph.enter_context(nc.psum_tensor("ps_acc", [128, 512], F32))
        ps_sum = ph.enter_context(nc.psum_tensor("ps_sum", [128, 512], F32))
        ps_tp = [ph.enter_context(nc.psum_tensor(f"ps_tp{j}", [128, 512], F32)) for j in range(2)]
        TPE = {"v": tpe_v, "g": tpe_g}
        BFB = {"v": bfvB, "g": bfgB}

        @blk.gpsimd
        def _(g):
            g.wait_ge(sv.h, PRO_V)
            g.tensor_tensor(brow[:1, 8:12], brow[:1, 0:4], brow[:1, 4:8], AOP.mult).then_inc(g2.h, 1)
            assert g2.inc() == PRO_G

            def e2ag(hh):
                g.wait_ge(*stE2.tok(hh * 16 + 15))
                g.collective_compute("AllGather", AOP.bypass,
                                     replica_groups=[list(range(NCORES))],
                                     ins=[e2in_h[hh].ap().opt()],
                                     outs=[e2out_h[hh].ap().opt()]).then_inc(hcc[hh].h, 1)
                hcc[hh].inc()
                g.collective_compute("AllGather", AOP.bypass,
                                     replica_groups=[list(range(NCORES))],
                                     ins=[e2sin[hh:hh + 1, :].opt()],
                                     outs=[e2sout_h[hh].ap().opt()]).then_inc(hcc[hh].h, 1)
                hcc[hh].inc()

            e2ag(0)
            for i, k in enumerate(GCH):
                g.wait_ge(*stTFg.tok(i))
                if i >= 1:
                    g.wait_ge(mm.h, TPdone[("u", GCH[i - 1])])
                sdb, ddb = bfgA, bfgB
                h = 1
                last = None
                while h < D:
                    nb = D // (2 * h)
                    a_ = bass.AP(sdb, 0, [[D, 128], [2 * h, nb], [1, h]])
                    b_ = bass.AP(sdb, h, [[D, 128], [2 * h, nb], [1, h]])
                    oa = bass.AP(ddb, 0, [[D, 128], [2 * h, nb], [1, h]])
                    ob = bass.AP(ddb, h, [[D, 128], [2 * h, nb], [1, h]])
                    g.tensor_tensor(oa, a_, b_, AOP.add)
                    last = g.tensor_tensor(ob, a_, b_, AOP.subtract)
                    sdb, ddb = ddb, sdb
                    h *= 2
                assert sdb is bfgB
                last.then_inc(g2.h, 1)
                assert g2.inc() == GBF[k]
                if i == 4:
                    e2ag(1)
            e2ag(2)
            e2ag(3)
            g.wait_ge(*stWrU.tok(15))
            g.wait_ge(*stWrD.tok(7))
            g.collective_compute("AllGather", AOP.bypass,
                                 replica_groups=[list(range(NCORES))],
                                 ins=[e3u_in.ap().opt()],
                                 outs=[e3u_out.ap().opt()]).then_inc(cc2.h, 1)
            cc2.inc()
            g.collective_compute("AllGather", AOP.bypass,
                                 replica_groups=[list(range(NCORES))],
                                 ins=[e3d_in.ap().opt()],
                                 outs=[e3d_out.ap().opt()]).then_inc(cc2.h, 1)
            cc2.inc()

        @blk.tensor
        def _(t):
            t.wait_ge(sv.h, PRO_V2)
            for h_ in range(4):
                t.matmul(ps_sum[:, h_:h_ + 1], ones_f1[:1, :], brow[:1, 8 + h_:9 + h_],
                         start=True, stop=True).then_inc(mm.h, 1)
                mm.inc()
            assert mm.n == PRO_MM
            qglob = [0]

            def emit_tp_groups(s_):
                for kind_, id_ in tp_by_slot.get(s_, []):
                    if kind_ == "u":
                        k = id_
                        pipe = PIPE[k]
                        if pipe == "v":
                            t.wait_ge(sv.h, BFVt[k])
                        else:
                            t.wait_ge(g2.h, GBF[k])
                        srcbuf = BFB[pipe]
                    else:
                        t.wait_ge(*stDL.tok(id_))
                        srcbuf = braw[id_ % 2]
                    for j in range(16):
                        q = qglob[0]
                        if q >= 2:
                            t.wait_ge(sa.h, AEVq[q - 2])
                        t.transpose(ps_tp[q % 2][:, 0:128], srcbuf[:, j * 128:(j + 1) * 128],
                                    ident_sb[:, :]).then_inc(mm.h, 1)
                        mm.inc()
                        qglob[0] += 1
                    assert TPdone[(kind_, id_)] == mm.n

            def emit_s(m):
                hh_m, qm_m, kc_m, _, _ = sched[m]
                if kc_m == 0 and qm_m == 0:
                    t.wait_ge(*stH.tok(hh_m))
                if m >= 4:
                    t.wait_ge(sa.h, EXPPt[m // 2 - 2])
                t.matmul(ps_s4[:, (m % 4) * 512:(m % 4 + 1) * 512],
                         kb[hh_m % 2][:, kc_m * 128:(kc_m + 1) * 128],
                         qb[hh_m % 2][:, qm_m * 512:(qm_m + 1) * 512],
                         start=True, stop=True).then_inc(mm.h, 1)
                assert SMMd[m] == mm.inc()

            emit_s(0)
            emit_s(1)
            for n, (hh, qm, kc, last_kc, dm) in enumerate(sched):
                gqm = hh * 16 + qm
                if n % 2 == 0:
                    if n + 2 < NSC:
                        emit_s(n + 2)
                    if n + 3 < NSC:
                        emit_s(n + 3)
                t.wait_ge(sa.h, EXPPt[n // 2])
                if kc == 0 and gqm >= 1:
                    t.wait_ge(sv.h, EPIt[gqm - 1])
                pr = n // 2
                half = (n % 2) * 512
                t.matmul(ps_acc[:, :], vb[hh % 2][:, kc * 128:(kc + 1) * 128],
                         pt[pr % 3][:, half:half + 512], start=(kc == 0), stop=last_kc).then_inc(mm.h, 1)
                mm.inc()
                t.matmul(ps_sum[:, :], ones128[:, :], pt[pr % 3][:, half:half + 512],
                         start=(kc == 0), stop=last_kc).then_inc(mm.h, 1)
                assert SUMMd[n] == mm.inc()
                if last_kc:
                    if qm == 15:
                        assert ATMMd[hh] == mm.n
                    emit_tp_groups(gqm)

        @blk.scalar
        def _(s):
            s.wait_ge(g2.h, PRO_G)
            s.sqrt(brow[:1, 12:16], brow[:1, 8:12]).then_inc(sa.h, 1)
            assert sa.inc() == PRO_A
            qglob = [0]
            wr_tok_u = {}
            wr_tok_d = {}

            def emit_ev_groups(s_):
                for kind_, id_ in tp_by_slot.get(s_, []):
                    if kind_ == "u":
                        pipe = PIPE[id_]
                        pidx = PIDX[id_]
                        tpe = TPE[pipe]
                        soff = (pidx % 2) * D
                        if pidx >= 2:
                            s.wait_ge(*stWrU.tok(PREV_U[id_]))
                    else:
                        tpe = tpe_d
                        soff = (id_ % 2) * D
                        if id_ >= 2:
                            s.wait_ge(*stWrD.tok(id_ - 2))
                    for j in range(16):
                        q = qglob[0]
                        s.wait_ge(mm.h, TPtile[q])
                        if kind_ == "u":
                            scl_ap = suu_sb[:, j:j + 1]
                        else:
                            dimblock = id_ // 2
                            scl_ap = sud_sb[:, dimblock * 16 + j: dimblock * 16 + j + 1]
                        s.activation(tpe[:, soff + j * 128: soff + (j + 1) * 128],
                                     ps_tp[q % 2][:, 0:128], AF.Copy, scale=scl_ap).then_inc(sa.h, 1)
                        got = sa.inc()
                        assert got == AEVq[q], (got, AEVq[q])
                        qglob[0] += 1

            for p in range(NPAIR):
                n0, n1 = 2 * p, 2 * p + 1
                hh, qm, kc1, last1, dm1 = sched[n1]
                _, _, kc0, _, dm0 = sched[n0]
                gqm = hh * 16 + qm
                if p == 0:
                    s.wait_ge(sv.h, PRO_V3)
                if dm1 is not None:
                    s.wait_ge(sv.h, MSKt[n1])
                else:
                    s.wait_ge(mm.h, SMMd[n1])
                if p >= 3:
                    s.wait_ge(mm.h, SUMMd[2 * (p - 3) + 1])
                s.activation(pt[p % 3][:, :], ps_s4[:, (n0 % 4) * 512:(n0 % 4) * 512 + 1024], AF.Exp,
                             bias=bias4[:, hh:hh + 1], scale=SCL).then_inc(sa.h, 1)
                assert sa.inc() == EXPPt[p]
                if last1:
                    emit_ev_groups(gqm)

        @blk.vector
        def _(v):
            for h_ in range(4):
                v.reduce_max(brow[:1, h_:h_ + 1], bquad[:1, h_ * 8:(h_ + 1) * 8], axis=AX.X).then_inc(sv.h, 1)
                sv.inc()
                v.reduce_max(brow[:1, 4 + h_:5 + h_], bquad[:1, 32 + h_ * 8: 32 + (h_ + 1) * 8],
                             axis=AX.X).then_inc(sv.h, 1)
                sv.inc()
            assert sv.n == PRO_V
            v.wait_ge(sa.h, PRO_A)
            v.tensor_scalar(brow[:1, 8:12], brow[:1, 12:16], -1.03 * SCL, -1.0,
                            AOP.mult, AOP.add).then_inc(sv.h, 1)
            assert sv.inc() == PRO_V2
            v.wait_ge(mm.h, PRO_MM)
            v.tensor_copy(bias4[:, 0:4], ps_sum[:, 0:4]).then_inc(sv.h, 1)
            assert sv.inc() == PRO_V3

            def emit_bfv(s_):
                for i, k in enumerate(VCH):
                    if slot_bfv[k] != s_:
                        continue
                    v.wait_ge(*stTFv.tok(i))
                    if i >= 1:
                        v.wait_ge(mm.h, TPdone[("u", VCH[i - 1])])
                    sdb, ddb = bfvA, bfvB
                    h = 1
                    last = None
                    while h < D:
                        nb = D // (2 * h)
                        a_ = bass.AP(sdb, 0, [[D, 128], [2 * h, nb], [1, h]])
                        b_ = bass.AP(sdb, h, [[D, 128], [2 * h, nb], [1, h]])
                        oa = bass.AP(ddb, 0, [[D, 128], [2 * h, nb], [1, h]])
                        ob = bass.AP(ddb, h, [[D, 128], [2 * h, nb], [1, h]])
                        v.tensor_tensor(oa, a_, b_, AOP.add)
                        last = v.tensor_tensor(ob, a_, b_, AOP.subtract)
                        sdb, ddb = ddb, sdb
                        h *= 2
                    assert sdb is bfvB
                    last.then_inc(sv.h, 1)
                    assert sv.inc() == BFVt[k]

            for n, (hh, qm, kc, last_kc, dm) in enumerate(sched):
                gqm = hh * 16 + qm
                if dm is not None:
                    v.wait_ge(mm.h, SMMd[n])
                    sl = (n % 4) * 512
                    v.tensor_tensor(ps_s4[:, sl:sl + 512], ps_s4[:, sl:sl + 512],
                                    mask_sb[:, 384 - 128 * dm: 896 - 128 * dm], AOP.add).then_inc(sv.h, 1)
                    assert sv.inc() == MSKt[n]
                if last_kc:
                    v.wait_ge(mm.h, SUMMd[n])
                    if gqm >= 2:
                        v.wait_ge(*stE2.tok(gqm - 2))
                    v.tensor_copy(osb[qm % 2][:, :], ps_acc[:, :]).then_inc(sv.h, 1)
                    sv.inc()
                    v.tensor_copy(ssb[qm % 2][:1, :], ps_sum[0:1, :]).then_inc(sv.h, 1)
                    assert sv.inc() == EPIt[gqm]
                    emit_bfv(gqm)

        @blk.sync
        def _(sy):
            sy.wait_ge(*stQs.tok(31))
            sy.wait_ge(*stKs.tok(31))
            sy.wait_ge(*stVs.tok(63))
            sy.wait_ge(*stRW.tok(1))

            def head_loads(hh):
                stH.issue(sy.dma_start(out=qb[hh % 2][:, :], in_=ap2(qs, hh * 128, 128, 0, T)))
                stH.issue(sy.dma_start(out=kb[hh % 2][:, :], in_=ap2(ks, hh * 128, 128, 0, T)))
                stH.issue(sy.dma_start(out=vb[hh % 2][:, :], in_=ap3(vs[hh], 0, 0, 128, 128 * 128, 64, 128)))

            def tf_loads(s_):
                for k in range(16):
                    if slot_ld[k] != s_:
                        continue
                    kind = "u" if k < 8 else "g"
                    c = k % 8
                    srcw = wup_r if kind == "u" else wgt_r
                    pipe = PIPE[k]
                    pidx = PIDX[k]
                    stf = stTFv if pipe == "v" else stTFg
                    dstb = bfvA if pipe == "v" else bfgA
                    if pidx >= 1:
                        prevk = (VCH if pipe == "v" else GCH)[pidx - 1]
                        sy.wait_ge(mm.h, TPdone[("u", prevk)])
                    stf.issue(sy.dma_start(out=dstb[:, :], in_=ap2(srcw, c * 128, 128, 0, D)))
                for g_ in range(8):
                    if slot_ld.get(("d", g_)) != s_:
                        continue
                    if g_ >= 2:
                        sy.wait_ge(mm.h, TPdone[("d", g_ - 2)])
                    rc, dimblock = g_ % 2, g_ // 2
                    stDL.issue(sy.dma_start(out=braw[g_ % 2][:, :],
                                            in_=ap2(raw_d, rc * 128, 128, dimblock * D, D)))

            def tf_writes(s_):
                for k in range(16):
                    if slot_wr[k] != s_:
                        continue
                    pipe = PIPE[k]
                    pidx = PIDX[k]
                    tpe = TPE[pipe]
                    soff = (pidx % 2) * D
                    sy.wait_ge(sa.h, AEV_LAST[("u", k)])
                    col0 = (0 if k < 8 else 1024) + (k % 8) * 128
                    stWrU.issue(sy.dma_start(
                        out=ap3(e3u_in, 0, col0, 128, 128 * 2048, 16, 128),
                        in_=sb3(tpe, soff, 128, 16, 128)))
                for g_ in range(8):
                    if slot_wr_d[g_] != s_:
                        continue
                    sy.wait_ge(sa.h, AEV_LAST[("d", g_)])
                    rc, dimblock = g_ % 2, g_ // 2
                    soff = (g_ % 2) * D
                    stWrD.issue(sy.dma_start(
                        out=ap3(e3d_in, dimblock * 2048, rc * 128, 128, 128 * 256, 16, 128),
                        in_=sb3(tpe_d, soff, 128, 16, 128)))

            tf_loads(0)
            head_loads(0)
            for hh in range(HPC):
                for qm in range(16):
                    gqm = hh * 16 + qm
                    if qm == 2 and hh + 1 < HPC:
                        if hh >= 1:
                            sy.wait_ge(mm.h, ATMMd[hh - 1])
                        head_loads(hh + 1)
                    if gqm >= 1:
                        tf_loads(gqm)
                    sy.wait_ge(sv.h, EPIt[gqm])
                    stE2.issue(sy.dma_start(out=ap2(e2in_h[hh], 0, 128, qm * 512, 512), in_=osb[qm % 2][:, :]))
                    stE2.issue(sy.dma_start(out=ap2(e2sin, hh, 1, qm * 512, 512), in_=ssb[qm % 2][:1, :]))
                    tf_writes(gqm)

    E2_CC = cc.n
    E3_CC = cc2.n
    if stages <= 3:
        return _finish(3)

    V0, MM0, A0 = sv.n, mm.n, sa.n
    stSum = mk([GP[2]])
    stAtn = mk([GP[3], GP[4]])
    stWo = mk([GP[5], GP[0]])
    stX = mk([SY[0], SY[1]])
    stX1 = mk([SY[2], SY[3]])
    SUMI = V0 + 1
    RCPh = lambda h: V0 + 2 + 5 * h
    ATN = lambda i: V0 + 3 + 5 * (i // 4) + (i % 4)
    X1V = lambda o: V0 + 21 + 3 * o + 1
    XAC = lambda o: V0 + 21 + 3 * o + 2
    X2A = lambda o: V0 + 21 + 3 * o + 3
    V_C2 = V0 + 21 + 48 + 1
    RECIP2 = V0 + 21 + 48 + 2
    SQ5 = lambda o: A0 + o + 1
    SQT2 = A0 + 17
    MMRB = lambda i: MM0 + i + 1
    WOZ = lambda o: MM0 + 16 + 3 * o + 1
    XSM = lambda o: MM0 + 16 + 3 * o + 2
    X2M = lambda o: MM0 + 16 + 3 * o + 3

    ln2_rows = contextlib.ExitStack()
    s_r2 = ln2_rows.enter_context(nc.sbuf_tensor("ln2_sr", [1, TC], F32))
    xsum_sb = ln2_rows.enter_context(nc.sbuf_tensor("xsum_sb", [1, TC], F32))
    x2sum_sb = ln2_rows.enter_context(nc.sbuf_tensor("x2sum_sb", [1, TC], F32))

    with nc.Block() as blk, contextlib.ExitStack() as ph:
        sums16 = ph.enter_context(nc.sbuf_tensor("sums16", [16, TC], F32))
        recip16 = ph.enter_context(nc.sbuf_tensor("recip16", [16, TC], F32))
        tmpa = [ph.enter_context(nc.sbuf_tensor(f"tmpa{p}", [128, 4 * TC], BF16)) for p in range(2)]
        attn_sb = ph.enter_context(nc.sbuf_tensor("attn_sb", [128, 16 * TC], BF16))
        wo_sb = [ph.enter_context(nc.sbuf_tensor(f"wo_sb{p}", [128, 16 * 128], BF16)) for p in range(2)]
        xcb = [ph.enter_context(nc.sbuf_tensor(f"xcb{p}", [128, TC], F32)) for p in range(2)]
        x1b = [ph.enter_context(nc.sbuf_tensor(f"x1b{p}", [128, TC], F32)) for p in range(2)]
        sq5 = ph.enter_context(nc.sbuf_tensor("sq5", [128, TC], F32))
        ps_misc = ph.enter_context(nc.psum_tensor("ps_misc", [128, TC], F32))
        ps_z = ph.enter_context(nc.psum_tensor("ps_z", [128, TC], F32))

        @blk.gpsimd
        def _(g):
            with g.register("r1") as r1, g.register("r2") as r2:
                g.reg_load(r2, tok_sb[:1, 3:4])
                g.reg_load(r1, tok_sb[:1, 2:3])
                for hh in range(4):
                    g.wait_ge(hcc[hh].h, 2)
                    if hh == 0:
                        g.wait_ge(sv.h, SUMI)
                    stSum.issue(g.dma_start(out=sums16[4 * hh:4 * hh + 4, :],
                                            in_=bass.AP(e2sout_h[hh], r2, [[T, 4], [1, TC]])))
                    if hh >= 2:
                        g.wait_ge(sv.h, ATN(4 * (hh - 2) + 3))
                    stAtn.issue(g.dma_start(
                        out=tmpa[hh % 2][:, :],
                        in_=bass.AP(e2out_h[hh], r1, [[T, 128], [128 * T, 4], [1, TC]])))
            for o in range(16):
                if o >= 2:
                    g.wait_ge(mm.h, WOZ(o - 2))
                stWo.issue(g.dma_start(out=wo_sb[o % 2][:, :], in_=ap3(woT, 0, o * 128, 128, 128 * D, 16, 128)))

        @blk.tensor
        def _(t):
            for j in range(16):
                t.wait_ge(sv.h, RCPh(j // 4))
                if j >= 1:
                    t.wait_ge(sv.h, ATN(j - 1))
                last = None
                for c4 in range(4):
                    last = t.matmul(ps_misc[:, c4 * 512:(c4 + 1) * 512],
                                    sel_sb[:, j * 128:(j + 1) * 128],
                                    recip16[:, c4 * 512:(c4 + 1) * 512], start=True, stop=True)
                last.then_inc(mm.h, 1)
                assert mm.inc() == MMRB(j)
            for o in range(17):
                if o >= 1:
                    op = o - 1
                    t.wait_ge(sv.h, X1V(op))
                    if op >= 1:
                        t.wait_ge(sv.h, X2A(op - 1))
                    last = None
                    for c4 in range(4):
                        last = t.matmul(ps_misc[0:1, c4 * 512:(c4 + 1) * 512], ones_fc[:, :1],
                                        x1b[op % 2][:, c4 * 512:(c4 + 1) * 512], start=True, stop=True)
                    last.then_inc(mm.h, 1)
                    assert mm.inc() == XSM(op)
                    t.wait_ge(sa.h, SQ5(op))
                    t.wait_ge(sv.h, XAC(op))
                    last = None
                    for c4 in range(4):
                        last = t.matmul(ps_misc[0:1, c4 * 512:(c4 + 1) * 512], ones_fc[:, :1],
                                        sq5[:, c4 * 512:(c4 + 1) * 512], start=True, stop=True)
                    last.then_inc(mm.h, 1)
                    assert mm.inc() == X2M(op)
                if o < 16:
                    t.wait_ge(*stWo.tok(o))
                    if o == 0:
                        t.wait_ge(sv.h, ATN(15))
                    if o >= 1:
                        t.wait_ge(sv.h, X1V(o - 1))
                    last = None
                    for j in range(16):
                        hmod = 4 * (j % 4) + j // 4
                        for c4 in range(4):
                            last = t.matmul(ps_z[:, c4 * 512:(c4 + 1) * 512],
                                            wo_sb[o % 2][:, hmod * 128:(hmod + 1) * 128],
                                            attn_sb[:, j * TC + c4 * 512: j * TC + (c4 + 1) * 512],
                                            start=(j == 0), stop=(j == 15))
                    last.then_inc(mm.h, 1)
                    assert mm.inc() == WOZ(o)

        @blk.scalar
        def _(s):
            for o in range(16):
                s.wait_ge(sv.h, X1V(o))
                if o >= 1:
                    s.wait_ge(mm.h, X2M(o - 1))
                s.square(sq5[:, :], x1b[o % 2][:, :]).then_inc(sa.h, 1)
                assert sa.inc() == SQ5(o)
            s.wait_ge(sv.h, V_C2)
            s.activation(x2sum_sb[:1, :], x2sum_sb[:1, :], AF.Sqrt, bias=eps_sb[:1, :1]).then_inc(sa.h, 1)
            assert sa.inc() == SQT2

        @blk.vector
        def _(v):
            v.memset(sums16[:, :], 1.0).then_inc(sv.h, 1)
            assert sv.inc() == SUMI
            for hh in range(4):
                v.wait_ge(*stSum.tok(hh))
                v.tensor_scalar(recip16[:, :], sums16[:, :], 1e-30, None, AOP.add)
                v.reciprocal(recip16[:, :], recip16[:, :]).then_inc(sv.h, 1)
                assert sv.inc() == RCPh(hh)
                for j in range(4 * hh, 4 * hh + 4):
                    rk = j % 4
                    v.wait_ge(*stAtn.tok(hh))
                    v.wait_ge(mm.h, MMRB(j))
                    v.tensor_tensor(attn_sb[:, j * TC:(j + 1) * TC], tmpa[hh % 2][:, rk * TC:(rk + 1) * TC],
                                    ps_misc[:, :], AOP.mult).then_inc(sv.h, 1)
                    assert sv.inc() == ATN(j)
            for o in range(16):
                v.wait_ge(mm.h, WOZ(o))
                v.wait_ge(*stX.tok(o))
                if o >= 2:
                    v.wait_ge(*stX1.tok(o - 2))
                    v.wait_ge(mm.h, X2M(o - 2))
                v.tensor_tensor(x1b[o % 2][:, :], ps_z[:, :], xcb[o % 2][:, :], AOP.add).then_inc(sv.h, 1)
                assert sv.inc() == X1V(o)
                v.wait_ge(mm.h, XSM(o))
                if o == 0:
                    v.tensor_copy(xsum_sb[:1, :], ps_misc[0:1, :]).then_inc(sv.h, 1)
                else:
                    v.tensor_tensor(xsum_sb[:1, :], xsum_sb[:1, :], ps_misc[0:1, :], AOP.add).then_inc(sv.h, 1)
                assert sv.inc() == XAC(o)
                v.wait_ge(mm.h, X2M(o))
                if o == 0:
                    v.tensor_copy(x2sum_sb[:1, :], ps_misc[0:1, :]).then_inc(sv.h, 1)
                else:
                    v.tensor_tensor(x2sum_sb[:1, :], x2sum_sb[:1, :], ps_misc[0:1, :], AOP.add).then_inc(sv.h, 1)
                assert sv.inc() == X2A(o)
            v.tensor_scalar(xsum_sb[:1, :], xsum_sb[:1, :], 1.0 / D, None, AOP.mult)
            v.tensor_scalar(x2sum_sb[:1, :], x2sum_sb[:1, :], 1.0 / D, None, AOP.mult)
            v.tensor_tensor(s_r2[:1, :], xsum_sb[:1, :], xsum_sb[:1, :], AOP.mult)
            v.tensor_tensor(x2sum_sb[:1, :], x2sum_sb[:1, :], s_r2[:1, :], AOP.subtract).then_inc(sv.h, 1)
            assert sv.inc() == V_C2
            v.wait_ge(sa.h, SQT2)
            v.reciprocal(s_r2[:1, :], x2sum_sb[:1, :]).then_inc(sv.h, 1)
            assert sv.inc() == RECIP2

        @blk.sync
        def _(sy):
            for o in range(18):
                if o < 16:
                    if o >= 2:
                        sy.wait_ge(sv.h, X1V(o - 2))
                    stX.issue(sy.dma_start(out=xcb[o % 2][:, :], in_=ap2(x_fm, o * 128, 128, 0, TC)))
                if o >= 2:
                    sy.wait_ge(sv.h, X1V(o - 2))
                    stX1.issue(sy.dma_start(out=ap2(x1sp, (o - 2) * 128, 128, 0, TC),
                                            in_=x1b[(o - 2) % 2][:, :]))

    if stages <= 4:
        return _finish(4)

    V0, MM0 = sv.n, mm.n
    stL2 = mk([SY[4], SY[5]])
    stS2 = mk([SY[7], SY[8]])
    BC2 = MM0 + 1
    NRM2 = [V0 + i + 1 for i in range(16)]

    with nc.Block() as blk, contextlib.ExitStack() as ph:
        xc2 = [ph.enter_context(nc.sbuf_tensor(f"l2_xc{p}", [128, TC], F32)) for p in range(2)]
        xn2 = [ph.enter_context(nc.sbuf_tensor(f"l2_xn{p}", [128, TC], BF16)) for p in range(2)]
        t12 = ph.enter_context(nc.sbuf_tensor("l2_t1", [128, TC], F32))
        psm2 = ph.enter_context(nc.psum_tensor("l2_psm", [128, TC], F32))
        psr2 = ph.enter_context(nc.psum_tensor("l2_psr", [128, TC], F32))

        @blk.sync
        def _(sy):
            sy.wait_ge(*stX1.tok(15))
            for i in range(18):
                if i < 16:
                    if i >= 2:
                        sy.wait_ge(sv.h, NRM2[i - 2])
                    stL2.issue(sy.dma_start(out=xc2[i % 2][:, :], in_=ap2(x1sp, i * 128, 128, 0, TC)))
                if i >= 2:
                    sy.wait_ge(sv.h, NRM2[i - 2])
                    stS2.issue(sy.dma_start(out=ap2(u_dram, (i - 2) * 128, 128, 0, TC),
                                            in_=xn2[(i - 2) % 2][:, :]))

        @blk.tensor
        def _(t):
            t.wait_ge(sv.h, RECIP2)
            last = None
            for c4 in range(4):
                t.matmul(psm2[:, c4 * 512:(c4 + 1) * 512], ones_f1[:1, :],
                         xsum_sb[:1, c4 * 512:(c4 + 1) * 512], start=True, stop=True)
                last = t.matmul(psr2[:, c4 * 512:(c4 + 1) * 512], ones_f1[:1, :],
                                s_r2[:1, c4 * 512:(c4 + 1) * 512], start=True, stop=True)
            last.then_inc(mm.h, 1)
            assert mm.inc() == BC2

        @blk.vector
        def _(v):
            for i in range(16):
                v.wait_ge(*stL2.tok(i))
                v.wait_ge(mm.h, BC2)
                if i >= 2:
                    v.wait_ge(*stS2.tok(i - 2))
                v.tensor_tensor(t12[:, :], xc2[i % 2][:, :], psm2[:, :], AOP.subtract)
                v.tensor_tensor(xn2[i % 2][:, :], t12[:, :], psr2[:, :], AOP.mult).then_inc(sv.h, 1)
                assert sv.inc() == NRM2[i]

    ln2_rows.close()
    ln2_stS = stS2
    if stages <= 5:
        return _finish(5)

    V0, MM0, A0 = sv.n, mm.n, sa.n
    NJ = I8 // 128
    NR = D // 128
    stU = mk([SY[9]])
    stUG = mk([SY[10], SY[11]], gsize=2)
    stDW = mk([SY[6], SY[0]])
    stXc = mk([SY[1], SY[2]])
    stO = mk([SY[3], SY[4]])
    UMM = lambda tb, j: MM0 + 144 * tb + 2 * j + 1
    GMM = lambda tb, j: MM0 + 144 * tb + 2 * j + 2
    DMM = lambda tb, r: MM0 + 144 * tb + 128 + r + 1
    UEV = lambda tb, j: A0 + 128 * tb + 2 * j + 1
    GEV = lambda tb, j: A0 + 128 * tb + 2 * j + 2
    GPR = lambda tb, j: V0 + 80 * tb + j + 1
    OUTV = lambda tb, r: V0 + 80 * tb + 64 + r + 1

    def prev2(tb, k, nk):
        return (tb, k - 2) if k >= 2 else (tb - 1, nk - 2 + k)

    with nc.Block() as blk, contextlib.ExitStack() as ph:
        utb = ph.enter_context(nc.sbuf_tensor("utb", [128, 16 * TBLK], BF16))
        g_sb = ph.enter_context(nc.sbuf_tensor("g_sb", [128, 64 * TBLK], BF16))
        upt = [ph.enter_context(nc.sbuf_tensor(f"upt{p}", [128, 16 * 128], BF16)) for p in range(2)]
        gat = [ph.enter_context(nc.sbuf_tensor(f"gat{p}", [128, 16 * 128], BF16)) for p in range(2)]
        dwt = [ph.enter_context(nc.sbuf_tensor(f"dwt{p}", [128, 64 * 128], BF16)) for p in range(2)]
        tup = [ph.enter_context(nc.sbuf_tensor(f"tup{p}", [128, TBLK], F32)) for p in range(2)]
        tgt = [ph.enter_context(nc.sbuf_tensor(f"tgt{p}", [128, TBLK], F32)) for p in range(2)]
        x1c = [ph.enter_context(nc.sbuf_tensor(f"x1c{p}", [128, TBLK], F32)) for p in range(2)]
        outb = [ph.enter_context(nc.sbuf_tensor(f"outb{p}", [128, TBLK], F32)) for p in range(2)]
        ps_up = [ph.enter_context(nc.psum_tensor(f"ps_up{p}", [128, TBLK], F32)) for p in range(2)]
        ps_gt = [ph.enter_context(nc.psum_tensor(f"ps_gt{p}", [128, TBLK], F32)) for p in range(2)]
        ps_dn = [ph.enter_context(nc.psum_tensor(f"ps_dn{p}", [128, TBLK], F32)) for p in range(2)]

        @blk.sync
        def _(sy):
            sy.wait_ge(*ln2_stS.tok(15))
            sy.wait_ge(cc2.h, E3_CC)
            for tb in range(NTB):
                if tb >= 1:
                    sy.wait_ge(mm.h, GMM(tb - 1, NJ - 1))
                stU.issue(sy.dma_start(out=utb[:, :], in_=ap3(u_dram, 0, tb * TBLK, 128, 128 * TC, 16, TBLK)))
                for j in range(NJ):
                    if tb > 0 or j >= 2:
                        ptb, pj = prev2(tb, j, NJ)
                        sy.wait_ge(mm.h, GMM(ptb, pj))
                    rk, jl0 = j // 8, (j % 8) * 128
                    stUG.issue(sy.dma_start(out=upt[j % 2][:, :],
                                            in_=ap3(e3u_out, rk * D, jl0, 128, 128 * 2048, 16, 128)))
                    stUG.issue(sy.dma_start(out=gat[j % 2][:, :],
                                            in_=ap3(e3u_out, rk * D, 1024 + jl0, 128, 128 * 2048, 16, 128)))
                for r in range(NR + 2):
                    if r < NR:
                        if tb > 0 or r >= 2:
                            ptb, pr = prev2(tb, r, NR)
                            sy.wait_ge(mm.h, DMM(ptb, pr))
                        rkr, rl0 = r // 2, (r % 2) * 128
                        stDW.issue(sy.dma_start(out=dwt[r % 2][:, :],
                                                in_=ap3(e3d_out, rkr * I8, rl0, 128, 128 * 256, 64, 128)))
                        if tb > 0 or r >= 2:
                            ptb, pr = prev2(tb, r, NR)
                            sy.wait_ge(sv.h, OUTV(ptb, pr))
                        stXc.issue(sy.dma_start(out=x1c[r % 2][:, :],
                                                in_=ap2(x1sp, r * 128, 128, tb * TBLK, TBLK)))
                    if r >= 2:
                        sy.wait_ge(sv.h, OUTV(tb, r - 2))
                        stO.issue(sy.dma_start(out=ap2(out_fm, (r - 2) * 128, 128, tb * TBLK, TBLK),
                                               in_=outb[(r - 2) % 2][:, :]))
            sy.wait_ge(*stO.tok(NTB * NR - 1))

        @blk.tensor
        def _(t):
            t.wait_ge(cc2.h, E3_CC)
            for tb in range(NTB):
                t.wait_ge(*stU.tok(tb))
                for j in range(NJ):
                    t.wait_ge(*stUG.tok(64 * tb + j))
                    if tb > 0 or j >= 2:
                        ptb, pj = prev2(tb, j, NJ)
                        t.wait_ge(sa.h, UEV(ptb, pj))
                    last = None
                    for i in range(16):
                        last = t.matmul(ps_up[j % 2][:, :], upt[j % 2][:, i * 128:(i + 1) * 128],
                                        utb[:, i * TBLK:(i + 1) * TBLK], start=(i == 0), stop=(i == 15))
                    last.then_inc(mm.h, 1)
                    assert mm.inc() == UMM(tb, j)
                    if tb > 0 or j >= 2:
                        ptb, pj = prev2(tb, j, NJ)
                        t.wait_ge(sa.h, GEV(ptb, pj))
                    last = None
                    for i in range(16):
                        last = t.matmul(ps_gt[j % 2][:, :], gat[j % 2][:, i * 128:(i + 1) * 128],
                                        utb[:, i * TBLK:(i + 1) * TBLK], start=(i == 0), stop=(i == 15))
                    last.then_inc(mm.h, 1)
                    assert mm.inc() == GMM(tb, j)
                for r in range(NR):
                    t.wait_ge(*stDW.tok(16 * tb + r))
                    t.wait_ge(sv.h, GPR(tb, NJ - 1))
                    if tb > 0 or r >= 2:
                        ptb, pr = prev2(tb, r, NR)
                        t.wait_ge(sv.h, OUTV(ptb, pr))
                    last = None
                    for jc in range(NJ):
                        last = t.matmul(ps_dn[r % 2][:, :], dwt[r % 2][:, jc * 128:(jc + 1) * 128],
                                        g_sb[:, jc * TBLK:(jc + 1) * TBLK],
                                        start=(jc == 0), stop=(jc == NJ - 1))
                    last.then_inc(mm.h, 1)
                    assert mm.inc() == DMM(tb, r)

        @blk.scalar
        def _(s):
            for tb in range(NTB):
                for j in range(NJ):
                    s.wait_ge(mm.h, UMM(tb, j))
                    if tb > 0 or j >= 2:
                        ptb, pj = prev2(tb, j, NJ)
                        s.wait_ge(sv.h, GPR(ptb, pj))
                    s.activation(tup[j % 2][:, :], ps_up[j % 2][:, :], AF.Copy,
                                 scale=wsu_sb[:, j:j + 1]).then_inc(sa.h, 1)
                    assert sa.inc() == UEV(tb, j)
                    s.wait_ge(mm.h, GMM(tb, j))
                    s.activation(tgt[j % 2][:, :], ps_gt[j % 2][:, :], AF.Silu,
                                 scale=wsg_sb[:, j:j + 1]).then_inc(sa.h, 1)
                    assert sa.inc() == GEV(tb, j)

        @blk.vector
        def _(v):
            v.wait_ge(sv.h, WSC_V)
            for tb in range(NTB):
                for j in range(NJ):
                    v.wait_ge(sa.h, GEV(tb, j))
                    if tb >= 1:
                        v.wait_ge(mm.h, DMM(tb - 1, NR - 1))
                    v.tensor_tensor(g_sb[:, j * TBLK:(j + 1) * TBLK], tup[j % 2][:, :],
                                    tgt[j % 2][:, :], AOP.mult).then_inc(sv.h, 1)
                    assert sv.inc() == GPR(tb, j)
                for r in range(NR):
                    v.wait_ge(mm.h, DMM(tb, r))
                    v.wait_ge(*stXc.tok(16 * tb + r))
                    if tb > 0 or r >= 2:
                        ptb, pr = prev2(tb, r, NR)
                        v.wait_ge(*stO.tok(16 * ptb + pr))
                    v.scalar_tensor_tensor(outb[r % 2][:, :], ps_dn[r % 2][:, :],
                                           wsd_sb[:, r:r + 1], x1c[r % 2][:, :],
                                           AOP.mult, AOP.add).then_inc(sv.h, 1)
                    assert sv.inc() == OUTV(tb, r)

    return _finish(7)


def _host_constants():
    identm = np.eye(128, dtype=np.float32)
    kk = np.arange(128)[:, None]
    u = np.arange(896)[None, :]
    maskm = np.where(kk <= (u - 384), 0.0, -1.0e30).astype(np.float32)
    sel = np.zeros((16, 16 * 128), dtype=np.float32)
    for h in range(16):
        sel[h, h * 128:(h + 1) * 128] = 1.0
    return identm, maskm, sel


def kernel(L0_attn_x, L0_attn_Wq, L0_attn_Wk, L0_attn_Wv, L0_attn_Wo,
           L0_ffn_SU_ug, L0_ffn_SU_dp, L0_ffn_Wscale_ug, L0_ffn_Wscale_dp,
           L0_ffn_W_up, L0_ffn_W_gate, L0_ffn_W_down):
    x = np.asarray(L0_attn_x)
    Wq = np.asarray(L0_attn_Wq); Wk = np.asarray(L0_attn_Wk)
    Wv = np.asarray(L0_attn_Wv); Wo = np.asarray(L0_attn_Wo)
    SUu = np.asarray(L0_ffn_SU_ug); SUd = np.asarray(L0_ffn_SU_dp)
    Wsu = np.asarray(L0_ffn_Wscale_ug); Wsd = np.asarray(L0_ffn_Wscale_dp)
    Wup = np.asarray(L0_ffn_W_up); Wgt = np.asarray(L0_ffn_W_gate)
    Wdn = np.asarray(L0_ffn_W_down)

    identm, maskm, sel = _host_constants()
    WqT = np.ascontiguousarray(Wq.T); WkT = np.ascontiguousarray(Wk.T)
    WvT = np.ascontiguousarray(Wv.T); WoT = np.ascontiguousarray(Wo.T)
    su_ug2 = np.ascontiguousarray(SUu.reshape(16, 128).T)
    su_dp2 = np.ascontiguousarray(SUd.reshape(64, 128).T)
    wsc_up2 = np.ascontiguousarray(Wsu[:I8].reshape(64, 128).T)
    wsc_gt2 = np.ascontiguousarray(Wsu[I8:].reshape(64, 128).T)
    wsc_dp2 = np.ascontiguousarray(Wsd.reshape(16, 128).T)

    in_maps = []
    for c in range(NCORES):
        b, g = c // GS, c % GS
        hs = g * HPC * DH
        im = {
            "x_fm": np.ascontiguousarray(x[b, g * TC:(g + 1) * TC, :].T),
            "wqT": np.ascontiguousarray(WqT[:, hs:hs + HPC * DH]),
            "wkT": np.ascontiguousarray(WkT[:, hs:hs + HPC * DH]),
            "wvT": np.ascontiguousarray(WvT[:, hs:hs + HPC * DH]),
            "woT": WoT,
            "wup_r": np.ascontiguousarray(Wup[c * 1024:(c + 1) * 1024, :]),
            "wgt_r": np.ascontiguousarray(Wgt[c * 1024:(c + 1) * 1024, :]),
            "wdn_r": np.ascontiguousarray(Wdn[c * 256:(c + 1) * 256, :]),
            "su_ug": su_ug2, "su_dp": su_dp2,
            "wsc_up": wsc_up2, "wsc_gt": wsc_gt2, "wsc_dp": wsc_dp2,
            "ident": identm, "maskm": maskm, "sel16": sel,
            "tok0": np.array([[g * TC, b * GS * D * (TC // 2),
                               b * GS * DH * T + g * TC,
                               b * GS * T + g * TC]], dtype=np.int32),
        }
        in_maps.append(im)

    nc = build(debug=DEBUG, stages=STAGES)
    res = run_bass_kernel_spmd(nc, in_maps, core_ids=list(range(NCORES)))

    out = np.empty((B, T, D), dtype=np.float32)
    for c in range(NCORES):
        b, g = c // GS, c % GS
        out[b, g * TC:(g + 1) * TC, :] = res.results[c]["out_fm"].T
    if DEBUG:
        kernel._dbg = res.results
    return out


# version 12 (final)
# speedup vs baseline: 1.1461x; 1.1461x over previous
import contextlib
import numpy as np
import ml_dtypes
import concourse.bass as bass
import concourse.mybir as mybir
from concourse.bass_utils import run_bass_kernel_spmd

F32 = mybir.dt.float32
BF16 = mybir.dt.bfloat16
I32 = mybir.dt.int32
AF = mybir.ActivationFunctionType
AOP = mybir.AluOpType
AX = mybir.AxisListType

D = 2048
T = 8192
B = 2
TC = 2048
NH = 16
HPC = 4
DH = 128
I8 = 8192
NCORES = 8
GS = 4
SCL = 0.08838834764831845
ISQ_D = 0.022097086912079608
ISQ_I = 0.011048543456039804
TBLK = 512
NTB = TC // TBLK

DEBUG = False
STAGES = 99


class Ctr:
    def __init__(self, h):
        self.h = h
        self.n = 0

    def inc(self, k=1):
        self.n += k
        return self.n


class Stream:

    def __init__(self, sems, gsize=1):
        self.sems = sems
        self.k = len(sems)
        self.gsize = gsize
        self.bases = [s.n for s in sems]
        self.gi = 0
        self.di = 0

    def issue(self, instr):
        s = self.sems[self.gi % self.k]
        instr.then_inc(s.h, 16)
        s.inc(16)
        self.di += 1
        if self.di == self.gsize:
            self.di = 0
            self.gi += 1

    def tok(self, gidx):
        assert 0 <= gidx, gidx
        s = self.sems[gidx % self.k]
        return (s.h, self.bases[gidx % self.k] + 16 * self.gsize * (gidx // self.k + 1))

    def final(self):
        assert self.di == 0
        return self.tok(self.gi - 1) if self.gi > 0 else None


def ap2(t, r0, nr, c0, ncol):
    C = t.shape[1]
    return bass.AP(t, r0 * C + c0, [[C, nr], [1, ncol]])


def ap3(t, r0, c0, nr, cstep, nch, inner):
    C = t.shape[1]
    return bass.AP(t, r0 * C + c0, [[C, nr], [cstep, nch], [1, inner]])


def sb3(t, c0, cstep, nch, inner, np_=128):
    F = t.shape[1]
    return bass.AP(t, c0, [[F, np_], [cstep, nch], [1, inner]])


def build(debug=False, stages=99):
    nc = bass.Bass("TRN2", target_bir_lowering=False)

    x_fm = nc.dram_tensor("x_fm", [D, TC], F32, kind="ExternalInput")
    wqT = nc.dram_tensor("wqT", [D, HPC * DH], F32, kind="ExternalInput")
    wkT = nc.dram_tensor("wkT", [D, HPC * DH], F32, kind="ExternalInput")
    wvT = nc.dram_tensor("wvT", [D, HPC * DH], F32, kind="ExternalInput")
    woT = nc.dram_tensor("woT", [D, D], F32, kind="ExternalInput")
    wup_r = nc.dram_tensor("wup_r", [I8 // NCORES, D], F32, kind="ExternalInput")
    wgt_r = nc.dram_tensor("wgt_r", [I8 // NCORES, D], F32, kind="ExternalInput")
    wdn_r = nc.dram_tensor("wdn_r", [D // NCORES, I8], F32, kind="ExternalInput")
    su_ug = nc.dram_tensor("su_ug", [128, 16], F32, kind="ExternalInput")
    su_dp = nc.dram_tensor("su_dp", [128, 64], F32, kind="ExternalInput")
    wsc_up = nc.dram_tensor("wsc_up", [128, 64], F32, kind="ExternalInput")
    wsc_gt = nc.dram_tensor("wsc_gt", [128, 64], F32, kind="ExternalInput")
    wsc_dp = nc.dram_tensor("wsc_dp", [128, 16], F32, kind="ExternalInput")
    ident = nc.dram_tensor("ident", [128, 128], F32, kind="ExternalInput")
    maskm = nc.dram_tensor("maskm", [128, 896], F32, kind="ExternalInput")
    sel16 = nc.dram_tensor("sel16", [16, 16 * 128], F32, kind="ExternalInput")
    tok0 = nc.dram_tensor("tok0", [1, 4], I32, kind="ExternalInput")

    out_fm = nc.dram_tensor("out_fm", [D, TC], F32, kind="ExternalOutput")

    e1in_h = [nc.dram_tensor(f"e1in{h}", [D, TC // 2], BF16) for h in range(2)]
    e1out_h = [nc.dram_tensor(f"e1out{h}", [NCORES * D, TC // 2], BF16, addr_space="Shared") for h in range(2)]
    qs = nc.dram_tensor("qs", [HPC * DH, T], BF16)
    ks = nc.dram_tensor("ks", [HPC * DH, T], BF16)
    vs = [nc.dram_tensor(f"vs{h}", [T, DH], BF16) for h in range(HPC)]
    e2in_h = [nc.dram_tensor(f"e2in{h}", [DH, T], BF16) for h in range(HPC)]
    e2out_h = [nc.dram_tensor(f"e2out{h}", [NCORES * DH, T], BF16, addr_space="Shared") for h in range(HPC)]
    e2sin = nc.dram_tensor("e2sin", [HPC, T], F32)
    e2sout_h = [nc.dram_tensor(f"e2sout{h}", [NCORES, T], F32, addr_space="Shared") for h in range(HPC)]
    e3u_in = nc.dram_tensor("e3u_in", [D, 2048], BF16)
    e3u_out = nc.dram_tensor("e3u_out", [NCORES * D, 2048], BF16, addr_space="Shared")
    e3d_in = nc.dram_tensor("e3d_in", [I8, 256], BF16)
    e3d_out = nc.dram_tensor("e3d_out", [NCORES * I8, 256], BF16, addr_space="Shared")
    x1sp = nc.dram_tensor("x1sp", [D, TC], F32)
    u_dram = nc.dram_tensor("u_dram", [D, TC], BF16)

    dbg = {}
    if debug:
        for name, shp, dt in [
            ("d_e1", [NCORES * D, TC // 2], BF16), ("d_q", [HPC * DH, T], BF16),
            ("d_k", [HPC * DH, T], BF16), ("d_v0", [T, DH], BF16),
            ("d_e2", [NCORES * DH, T], BF16), ("d_e2s", [HPC, T], F32),
            ("d_x1", [D, TC], F32), ("d_u", [D, TC], BF16),
            ("d_e3u", [NCORES * D, 2048], BF16), ("d_e3d", [NCORES * I8, 256], BF16),
        ]:
            dbg[name] = nc.dram_tensor(name, shp, dt, kind="ExternalOutput")

    st = contextlib.ExitStack()
    NSY, NGP = 12, 6
    SY = [Ctr(st.enter_context(nc.semaphore(f"sy{i}"))) for i in range(NSY)]
    GP = [Ctr(st.enter_context(nc.semaphore(f"gp{i}"))) for i in range(NGP)]
    mm = Ctr(st.enter_context(nc.semaphore("mm")))
    sv = Ctr(st.enter_context(nc.semaphore("sv")))
    sa = Ctr(st.enter_context(nc.semaphore("sa")))
    cc = Ctr(st.enter_context(nc.semaphore("cc")))
    cc2 = Ctr(st.enter_context(nc.semaphore("cc2")))
    cc3 = Ctr(st.enter_context(nc.semaphore("cc3")))
    g2 = Ctr(st.enter_context(nc.semaphore("g2")))
    hcc = [Ctr(st.enter_context(nc.semaphore(f"hcc{h}"))) for h in range(HPC)]
    all_streams = []

    def mk(sems, gsize=1):
        s = Stream(sems, gsize)
        all_streams.append(s)
        return s

    ones_bf = st.enter_context(nc.sbuf_tensor("ones_bf", [128, 1], BF16))
    ones128 = st.enter_context(nc.sbuf_tensor("ones128", [128, 128], BF16))
    ones_f1 = st.enter_context(nc.sbuf_tensor("ones_f1", [1, 128], F32))
    ones_fc = st.enter_context(nc.sbuf_tensor("ones_fc", [128, 1], F32))
    ident_sb = st.enter_context(nc.sbuf_tensor("ident_sb", [128, 128], F32))
    mask_sb = st.enter_context(nc.sbuf_tensor("mask_sb", [128, 896], F32))
    sel_sb = st.enter_context(nc.sbuf_tensor("sel_sb", [16, 16 * 128], F32))
    suu_sb = st.enter_context(nc.sbuf_tensor("suu_sb", [128, 16], F32))
    sud_sb = st.enter_context(nc.sbuf_tensor("sud_sb", [128, 64], F32))
    wsu_sb = st.enter_context(nc.sbuf_tensor("wsu_sb", [128, 64], F32))
    wsg_sb = st.enter_context(nc.sbuf_tensor("wsg_sb", [128, 64], F32))
    wsd_sb = st.enter_context(nc.sbuf_tensor("wsd_sb", [128, 16], F32))
    tok_sb = st.enter_context(nc.sbuf_tensor("tok_sb", [1, 4], I32))
    bquad = st.enter_context(nc.sbuf_tensor("bquad", [1, 64], F32))
    brow = st.enter_context(nc.sbuf_tensor("brow", [1, 16], F32))
    bias4 = st.enter_context(nc.sbuf_tensor("bias4", [128, 4], F32))
    eps_sb = st.enter_context(nc.sbuf_tensor("eps_sb", [1, 1], F32))

    def _finish(upto):
        if debug:
            allpairs = [("d_e1", e1out_h[0], 1), ("d_q", qs, 2), ("d_k", ks, 2), ("d_v0", vs[0], 2),
                        ("d_e2", e2out_h[0], 3), ("d_e2s", e2sin, 3), ("d_x1", x1sp, 4), ("d_u", u_dram, 5),
                        ("d_e3u", e3u_out, 3), ("d_e3d", e3d_out, 3)]
            pairs = [(n_, s_) for n_, s_, k_ in allpairs if k_ <= upto]
            finals = [s_.final() for s_ in all_streams]
            stD = mk([SY[0]], gsize=len(pairs))
            with nc.Block() as blk:
                @blk.sync
                def _(sy):
                    for f in finals:
                        if f is not None:
                            sy.wait_ge(*f)
                    sy.wait_ge(cc.h, cc.n)
                    sy.wait_ge(cc2.h, cc2.n)
                    sy.wait_ge(cc3.h, cc3.n)
                    for h_ in range(HPC):
                        sy.wait_ge(hcc[h_].h, hcc[h_].n)
                    sy.wait_ge(sv.h, sv.n)
                    sy.wait_ge(sa.h, sa.n)
                    sy.wait_ge(mm.h, mm.n)
                    for name, src_ in pairs:
                        stD.issue(sy.dma_start(out=dbg[name].ap(), in_=src_.ap()))
                    f = stD.final()
                    if f is not None:
                        sy.wait_ge(*f)
        st.close()
        return nc

    raw_d = nc.dram_tensor("raw_d", [256, I8], F32)

    V0, MM0, A0, G0 = sv.n, mm.n, sa.n, g2.n
    stC = mk([SY[0]], gsize=9)
    stWd = mk([SY[1]])
    stWd2 = mk([GP[5]])
    stRW = mk([GP[4]])
    stL = mk([SY[2], SY[3]])
    stS = mk([SY[4], SY[5]])
    WSC_V = V0 + 1
    GBFD = [G0 + 2, G0 + 3]
    SQ = [A0 + i + 1 for i in range(16)]
    SQT = A0 + 17
    MMD = [MM0 + i + 1 for i in range(16)]
    BC = MM0 + 17
    V_COMB = V0 + 2
    RECIP = V0 + 3
    NRM = {}
    for _h in range(2):
        for _i in range(16):
            NRM[(_h, _i)] = V0 + 3 + 16 * _h + _i + 1

    with nc.Block() as blk, contextlib.ExitStack() as ph:
        bfdA = ph.enter_context(nc.sbuf_tensor("bfdA", [128, I8], F32))
        bfdB = ph.enter_context(nc.sbuf_tensor("bfdB", [128, I8], F32))
        xc = [ph.enter_context(nc.sbuf_tensor(f"ln1_xc{p}", [128, TC], F32)) for p in range(2)]
        sq = ph.enter_context(nc.sbuf_tensor("ln1_sq", [128, TC], F32))
        xn = [ph.enter_context(nc.sbuf_tensor(f"ln1_xn{p}", [128, TC], BF16)) for p in range(2)]
        t1 = ph.enter_context(nc.sbuf_tensor("ln1_t1", [128, TC], F32))
        s_m = ph.enter_context(nc.sbuf_tensor("ln1_sm", [1, TC], F32))
        s_r = ph.enter_context(nc.sbuf_tensor("ln1_sr", [1, TC], F32))
        s_t = ph.enter_context(nc.sbuf_tensor("ln1_st", [1, TC], F32))
        psm = ph.enter_context(nc.psum_tensor("ln1_psm", [128, TC], F32))
        psr = ph.enter_context(nc.psum_tensor("ln1_psr", [128, TC], F32))

        @blk.sync
        def _(sy):
            for tsrc, tdst in [(ident, ident_sb), (maskm, mask_sb), (sel16, sel_sb),
                               (su_ug, suu_sb), (su_dp, sud_sb), (wsc_up, wsu_sb),
                               (wsc_gt, wsg_sb), (wsc_dp, wsd_sb), (tok0, tok_sb)]:
                stC.issue(sy.dma_start(out=tdst[:, :], in_=tsrc[:, :]))
            stWd.issue(sy.dma_start(out=bfdA[:, :], in_=ap2(wdn_r, 0, 128, 0, I8)))
            for i in range(16):
                if i >= 2:
                    sy.wait_ge(mm.h, MMD[i - 2])
                stL.issue(sy.dma_start(out=xc[i % 2][:, :], in_=ap2(x_fm, i * 128, 128, 0, TC)))
            for h in range(2):
                for i in range(18):
                    if i < 16:
                        if 16 * h + i >= 2:
                            ph_, pi_ = divmod(16 * h + i - 2, 16)
                            sy.wait_ge(sv.h, NRM[(ph_, pi_)])
                        else:
                            sy.wait_ge(mm.h, MMD[15])
                        stL.issue(sy.dma_start(out=xc[i % 2][:, 0:TC // 2],
                                               in_=ap2(x_fm, i * 128, 128, h * (TC // 2), TC // 2)))
                    if i >= 2:
                        sy.wait_ge(sv.h, NRM[(h, i - 2)])
                        stS.issue(sy.dma_start(out=ap2(e1in_h[h], (i - 2) * 128, 128, 0, TC // 2),
                                               in_=xn[(i - 2) % 2][:, 0:TC // 2]))

        @blk.gpsimd
        def _(g):
            g.memset(ones_bf[:, :], 1.0)
            g.memset(ones128[:, :], 1.0)
            g.memset(ones_fc[:, :], 1.0)
            g.memset(eps_sb[:, :], 1e-5)
            g.memset(ones_f1[:, :], 1.0).then_inc(g2.h, 1)
            g2.inc()
            for i in range(2):
                g.wait_ge(*(stWd.tok(0) if i == 0 else stWd2.tok(0)))
                if i >= 1:
                    g.wait_ge(*stRW.tok(i - 1))
                sdb, ddb = bfdA, bfdB
                h = 1
                last = None
                while h < I8:
                    nb = I8 // (2 * h)
                    a_ = bass.AP(sdb, 0, [[I8, 128], [2 * h, nb], [1, h]])
                    b_ = bass.AP(sdb, h, [[I8, 128], [2 * h, nb], [1, h]])
                    oa = bass.AP(ddb, 0, [[I8, 128], [2 * h, nb], [1, h]])
                    ob = bass.AP(ddb, h, [[I8, 128], [2 * h, nb], [1, h]])
                    g.tensor_tensor(oa, a_, b_, AOP.add)
                    last = g.tensor_tensor(ob, a_, b_, AOP.subtract)
                    sdb, ddb = ddb, sdb
                    h *= 2
                assert sdb is bfdB
                last.then_inc(g2.h, 1)
                assert g2.inc() == GBFD[i]
                stRW.issue(g.dma_start(out=ap2(raw_d, i * 128, 128, 0, I8), in_=bfdB[:, :]))
                if i == 0:
                    stWd2.issue(g.dma_start(out=bfdA[:, :], in_=ap2(wdn_r, 128, 128, 0, I8)))
            g.wait_ge(*stS.tok(15))
            g.collective_compute("AllGather", AOP.bypass,
                                 replica_groups=[list(range(NCORES))],
                                 ins=[e1in_h[0].ap().opt()],
                                 outs=[e1out_h[0].ap().opt()]).then_inc(cc.h, 1)
            cc.inc()
            g.wait_ge(*stS.tok(31))
            g.collective_compute("AllGather", AOP.bypass,
                                 replica_groups=[list(range(NCORES))],
                                 ins=[e1in_h[1].ap().opt()],
                                 outs=[e1out_h[1].ap().opt()]).then_inc(cc3.h, 1)
            cc3.inc()

        @blk.scalar
        def _(s):
            for i in range(16):
                s.wait_ge(*stL.tok(i))
                if i >= 1:
                    s.wait_ge(mm.h, MMD[i - 1])
                s.square(sq[:, :], xc[i % 2][:, :]).then_inc(sa.h, 1)
                assert sa.inc() == SQ[i]
            s.wait_ge(sv.h, V_COMB)
            s.wait_ge(g2.h, G0 + 1)
            s.activation(s_t[:1, :], s_t[:1, :], AF.Sqrt, bias=eps_sb[:1, :1]).then_inc(sa.h, 1)
            assert sa.inc() == SQT

        @blk.tensor
        def _(t):
            t.wait_ge(g2.h, G0 + 1)
            for i in range(16):
                t.wait_ge(sa.h, SQ[i])
                last = None
                for c4 in range(4):
                    t.matmul(psm[0:1, c4 * 512:(c4 + 1) * 512], ones_fc[:, :1],
                             xc[i % 2][:, c4 * 512:(c4 + 1) * 512],
                             start=(i == 0), stop=(i == 15))
                    last = t.matmul(psr[0:1, c4 * 512:(c4 + 1) * 512], ones_fc[:, :1],
                                    sq[:, c4 * 512:(c4 + 1) * 512],
                                    start=(i == 0), stop=(i == 15))
                last.then_inc(mm.h, 1)
                assert mm.inc() == MMD[i]
            t.wait_ge(sv.h, RECIP)
            last = None
            for c4 in range(4):
                t.matmul(psm[:, c4 * 512:(c4 + 1) * 512], ones_f1[:1, :],
                         s_m[:1, c4 * 512:(c4 + 1) * 512], start=True, stop=True)
                last = t.matmul(psr[:, c4 * 512:(c4 + 1) * 512], ones_f1[:1, :],
                                s_r[:1, c4 * 512:(c4 + 1) * 512], start=True, stop=True)
            last.then_inc(mm.h, 1)
            assert mm.inc() == BC

        @blk.vector
        def _(v):
            v.wait_ge(*stC.tok(0))
            v.tensor_scalar(wsu_sb[:, :], wsu_sb[:, :], ISQ_D, None, AOP.mult)
            v.tensor_scalar(wsg_sb[:, :], wsg_sb[:, :], ISQ_D, None, AOP.mult)
            v.tensor_scalar(wsd_sb[:, :], wsd_sb[:, :], ISQ_I, None, AOP.mult).then_inc(sv.h, 1)
            assert sv.inc() == WSC_V
            v.wait_ge(mm.h, MMD[15])
            v.tensor_scalar(s_m[:1, :], psm[0:1, :], 1.0 / D, None, AOP.mult)
            v.tensor_scalar(s_t[:1, :], psr[0:1, :], 1.0 / D, None, AOP.mult)
            v.tensor_tensor(s_r[:1, :], s_m[:1, :], s_m[:1, :], AOP.mult)
            v.tensor_tensor(s_t[:1, :], s_t[:1, :], s_r[:1, :], AOP.subtract).then_inc(sv.h, 1)
            assert sv.inc() == V_COMB
            v.wait_ge(sa.h, SQT)
            v.reciprocal(s_r[:1, :], s_t[:1, :]).then_inc(sv.h, 1)
            assert sv.inc() == RECIP
            for h in range(2):
                for i in range(16):
                    v.wait_ge(*stL.tok(16 + 16 * h + i))
                    v.wait_ge(mm.h, BC)
                    if 16 * h + i >= 2:
                        v.wait_ge(*stS.tok(16 * h + i - 2))
                    hoff = h * (TC // 2)
                    v.tensor_tensor(t1[:, 0:TC // 2], xc[i % 2][:, 0:TC // 2],
                                    psm[:, hoff:hoff + TC // 2], AOP.subtract)
                    v.tensor_tensor(xn[i % 2][:, 0:TC // 2], t1[:, 0:TC // 2],
                                    psr[:, hoff:hoff + TC // 2], AOP.mult).then_inc(sv.h, 1)
                    assert sv.inc() == NRM[(h, i)]

    E1_CC = cc.n
    ln1_stS = stS
    if stages <= 1:
        return _finish(1)

    V0, MM0, A0 = sv.n, mm.n, sa.n
    NW = 8
    stWin = mk([GP[2], GP[3]])
    stQs = mk([SY[2], SY[3]])
    stKs = mk([SY[4], SY[5]])
    stVs = mk([SY[7], SY[8]], gsize=4)
    stWq = mk([GP[1]], gsize=3)
    MM_W = [MM0 + 24 * (w + 1) for w in range(NW)]
    QMM = lambda w, oq: MM0 + 24 * w + oq + 1
    KMM = lambda w, oq: MM0 + 24 * w + 4 + oq + 1
    VMM = lambda w, ts: MM0 + 24 * w + 8 + ts + 1
    BQMM = lambda w, oq: MM0 + 24 * w + 16 + oq + 1
    BKMM = lambda w, oq: MM0 + 24 * w + 20 + oq + 1
    QEV = lambda w, oq: A0 + 24 * w + 2 * oq + 1
    QSQ = lambda w, oq: A0 + 24 * w + 2 * oq + 2
    KEV = lambda w, oq: A0 + 24 * w + 8 + 2 * oq + 1
    KSQ = lambda w, oq: A0 + 24 * w + 8 + 2 * oq + 2
    VEV = lambda w, ts: A0 + 24 * w + 16 + ts + 1
    RBQ = lambda w, oq: V0 + 8 * w + oq + 1
    RBK = lambda w, oq: V0 + 8 * w + 4 + oq + 1

    with nc.Block() as blk, contextlib.ExitStack() as ph:
        win = [ph.enter_context(nc.sbuf_tensor(f"win{p}", [128, 16 * 1024], BF16)) for p in range(2)]
        wq_sb = ph.enter_context(nc.sbuf_tensor("wq_sb", [128, 16 * 512], BF16))
        wk_sb = ph.enter_context(nc.sbuf_tensor("wk_sb", [128, 16 * 512], BF16))
        wv_sb = ph.enter_context(nc.sbuf_tensor("wv_sb", [128, 16 * 512], BF16))
        qe = [ph.enter_context(nc.sbuf_tensor(f"qe{p}", [128, 1024], BF16)) for p in range(2)]
        ke = [ph.enter_context(nc.sbuf_tensor(f"ke{p}", [128, 1024], BF16)) for p in range(2)]
        ve = [ph.enter_context(nc.sbuf_tensor(f"ve{p}", [128, 512], BF16)) for p in range(2)]
        bsq = ph.enter_context(nc.sbuf_tensor("bsq", [128, 8 * 1024], BF16))
        ps_q = ph.enter_context(nc.psum_tensor("ps_q", [128, 1024], F32))
        ps_k = ph.enter_context(nc.psum_tensor("ps_k", [128, 1024], F32))
        ps_v = ph.enter_context(nc.psum_tensor("ps_v", [128, 512], F32))
        ps_b = ph.enter_context(nc.psum_tensor("ps_b", [1, 1024], F32))

        @blk.gpsimd
        def _(g):
            for wsrc, wdst in [(wqT, wq_sb), (wkT, wk_sb), (wvT, wv_sb)]:
                stWq.issue(g.dma_start(out=wdst[:, :], in_=ap3(wsrc, 0, 0, 128, 128 * 512, 16, 512)))
            g.wait_ge(cc.h, E1_CC)
            with g.register("rw0") as rw0, g.register("rw1") as rw1:
                g.reg_load(rw0, tok_sb[:1, 1:2])
                for w in range(NW):
                    rk, hhalf = w // 2, w % 2
                    if hhalf == 1:
                        g.wait_ge(cc3.h, 1)
                    if w >= 2:
                        g.wait_ge(mm.h, MM_W[w - 2])
                    g.reg_add(rw1, rw0, rk * D * (TC // 2))
                    stWin.issue(g.dma_start(
                        out=win[w % 2][:, :],
                        in_=bass.AP(e1out_h[hhalf], rw1,
                                    [[TC // 2, 128], [128 * (TC // 2), 16], [1, 1024]])))

        @blk.sync
        def _(sy):
            for w in range(NW):
                for oq in range(4):
                    sy.wait_ge(sa.h, QEV(w, oq))
                    stQs.issue(sy.dma_start(out=ap2(qs, oq * 128, 128, w * 1024, 1024), in_=qe[oq % 2][:, :]))
                for oq in range(4):
                    sy.wait_ge(sa.h, KEV(w, oq))
                    stKs.issue(sy.dma_start(out=ap2(ks, oq * 128, 128, w * 1024, 1024), in_=ke[oq % 2][:, :]))
                for ts in range(8):
                    sy.wait_ge(sa.h, VEV(w, ts))
                    for hh in range(4):
                        stVs.issue(sy.dma_start(out=ap2(vs[hh], w * 1024 + ts * 128, 128, 0, 128),
                                                in_=ve[ts % 2][:, hh * 128:(hh + 1) * 128]))

        @blk.tensor
        def _(t):
            t.wait_ge(*stWq.tok(0))
            for w in range(NW):
                t.wait_ge(*stWin.tok(w))
                for oq in range(4):
                    if w > 0 or oq >= 1:
                        t.wait_ge(sa.h, QEV(w, oq - 1) if oq >= 1 else QEV(w - 1, 3))
                    last = None
                    for i in range(16):
                        for c2 in range(2):
                            last = t.matmul(ps_q[:, c2 * 512:(c2 + 1) * 512],
                                            wq_sb[:, i * 512 + oq * 128: i * 512 + oq * 128 + 128],
                                            win[w % 2][:, i * 1024 + c2 * 512: i * 1024 + (c2 + 1) * 512],
                                            start=(i == 0), stop=(i == 15))
                    last.then_inc(mm.h, 1)
                    assert mm.inc() == QMM(w, oq)
                for oq in range(4):
                    if w > 0 or oq >= 1:
                        t.wait_ge(sa.h, KEV(w, oq - 1) if oq >= 1 else KEV(w - 1, 3))
                    last = None
                    for i in range(16):
                        for c2 in range(2):
                            last = t.matmul(ps_k[:, c2 * 512:(c2 + 1) * 512],
                                            wk_sb[:, i * 512 + oq * 128: i * 512 + oq * 128 + 128],
                                            win[w % 2][:, i * 1024 + c2 * 512: i * 1024 + (c2 + 1) * 512],
                                            start=(i == 0), stop=(i == 15))
                    last.then_inc(mm.h, 1)
                    assert mm.inc() == KMM(w, oq)
                for ts in range(8):
                    if w > 0 or ts >= 1:
                        t.wait_ge(sa.h, VEV(w, ts - 1) if ts >= 1 else VEV(w - 1, 7))
                    last = None
                    for i in range(16):
                        last = t.matmul(ps_v[:, :],
                                        win[w % 2][:, i * 1024 + ts * 128: i * 1024 + ts * 128 + 128],
                                        wv_sb[:, i * 512:(i + 1) * 512],
                                        start=(i == 0), stop=(i == 15))
                    last.then_inc(mm.h, 1)
                    assert mm.inc() == VMM(w, ts)
                for oq in range(4):
                    t.wait_ge(sa.h, QSQ(w, oq))
                    if w > 0 or oq >= 1:
                        t.wait_ge(sv.h, RBK(w - 1, 3) if oq == 0 else RBQ(w, oq - 1))
                    c0 = oq * 1024
                    t.matmul(ps_b[:1, 0:512], ones_bf[:, :], bsq[:, c0:c0 + 512], start=True, stop=True)
                    t.matmul(ps_b[:1, 512:1024], ones_bf[:, :], bsq[:, c0 + 512:c0 + 1024],
                             start=True, stop=True).then_inc(mm.h, 1)
                    assert mm.inc() == BQMM(w, oq)
                for oq in range(4):
                    t.wait_ge(sa.h, KSQ(w, oq))
                    t.wait_ge(sv.h, RBQ(w, 3) if oq == 0 else RBK(w, oq - 1))
                    c0 = (4 + oq) * 1024
                    t.matmul(ps_b[:1, 0:512], ones_bf[:, :], bsq[:, c0:c0 + 512], start=True, stop=True)
                    t.matmul(ps_b[:1, 512:1024], ones_bf[:, :], bsq[:, c0 + 512:c0 + 1024],
                             start=True, stop=True).then_inc(mm.h, 1)
                    assert mm.inc() == BKMM(w, oq)

        @blk.scalar
        def _(s):
            lastq = {0: None, 1: None}
            lastk = {0: None, 1: None}
            lastv = {0: None, 1: None}
            for w in range(NW):
                for oq in range(4):
                    s.wait_ge(mm.h, QMM(w, oq))
                    if lastq[oq % 2] is not None:
                        s.wait_ge(*lastq[oq % 2])
                    s.copy(qe[oq % 2][:, :], ps_q[:, :]).then_inc(sa.h, 1)
                    assert sa.inc() == QEV(w, oq)
                    s.square(bsq[:, oq * 1024:(oq + 1) * 1024], qe[oq % 2][:, :]).then_inc(sa.h, 1)
                    assert sa.inc() == QSQ(w, oq)
                    lastq[oq % 2] = stQs.tok(4 * w + oq)
                for oq in range(4):
                    s.wait_ge(mm.h, KMM(w, oq))
                    if lastk[oq % 2] is not None:
                        s.wait_ge(*lastk[oq % 2])
                    s.copy(ke[oq % 2][:, :], ps_k[:, :]).then_inc(sa.h, 1)
                    assert sa.inc() == KEV(w, oq)
                    s.square(bsq[:, (4 + oq) * 1024:(5 + oq) * 1024], ke[oq % 2][:, :]).then_inc(sa.h, 1)
                    assert sa.inc() == KSQ(w, oq)
                    lastk[oq % 2] = stKs.tok(4 * w + oq)
                for ts in range(8):
                    s.wait_ge(mm.h, VMM(w, ts))
                    if lastv[ts % 2] is not None:
                        s.wait_ge(*lastv[ts % 2])
                    s.copy(ve[ts % 2][:, :], ps_v[:, :]).then_inc(sa.h, 1)
                    assert sa.inc() == VEV(w, ts)
                    lastv[ts % 2] = stVs.tok(8 * w + ts)

        @blk.vector
        def _(v):
            for w in range(NW):
                for oq in range(4):
                    v.wait_ge(mm.h, BQMM(w, oq))
                    v.reduce_max(bquad[:1, oq * 8 + w: oq * 8 + w + 1], ps_b[:1, :],
                                 axis=AX.X).then_inc(sv.h, 1)
                    assert sv.inc() == RBQ(w, oq)
                for oq in range(4):
                    v.wait_ge(mm.h, BKMM(w, oq))
                    v.reduce_max(bquad[:1, 32 + oq * 8 + w: 32 + oq * 8 + w + 1], ps_b[:1, :],
                                 axis=AX.X).then_inc(sv.h, 1)
                    assert sv.inc() == RBK(w, oq)

    if stages <= 2:
        return _finish(2)

    V0, MM0, A0, G0 = sv.n, mm.n, sa.n, g2.n
    stH = mk([SY[9], SY[10]], gsize=3)
    stE2 = mk([SY[11], SY[6]], gsize=2)
    stTFv = mk([SY[0]])
    stTFg = mk([SY[1]])
    stWrU = mk([SY[2], SY[3]])
    stDL = mk([SY[4], SY[5]])
    stWrD = mk([SY[7], SY[8]])

    PRO_G = G0 + 1
    PRO_A = A0 + 1
    PRO_MM = MM0 + 4
    sched = []
    for hh in range(HPC):
        for qm in range(16):
            nk = 4 * qm + 4
            for kc in range(nk):
                sched.append((hh, qm, kc, kc == nk - 1, kc - 4 * qm if kc >= 4 * qm else None))
    NSC = len(sched)
    NPAIR = NSC // 2

    VCH = [0, 2, 4, 6, 8, 10, 12, 14]
    GCH = [1, 3, 5, 7, 9, 11, 13, 15]
    PIPE = {k: ("v" if k in VCH else "g") for k in range(16)}
    PIDX = {}
    for i, k in enumerate(VCH):
        PIDX[k] = i
    for i, k in enumerate(GCH):
        PIDX[k] = i
    slot_ld = {k: k + 4 for k in range(16)}
    slot_bfv = {k: k + 6 for k in VCH}
    slot_tp = {("u", k): k + 7 for k in range(16)}
    for g_ in range(8):
        slot_ld[("d", g_)] = 26 + g_
        slot_tp[("d", g_)] = 27 + g_
    slot_wr = {k: k + 8 for k in range(16)}
    slot_wr_d = {g_: 28 + g_ for g_ in range(8)}
    tp_groups = sorted(
        [("u", k, slot_tp[("u", k)]) for k in range(16)]
        + [("d", g_, slot_tp[("d", g_)]) for g_ in range(8)],
        key=lambda x: (x[2], 0 if x[0] == "u" else 1, x[1]))
    tp_by_slot = {}
    for kind_, id_, s_ in tp_groups:
        tp_by_slot.setdefault(s_, []).append((kind_, id_))
    GBF = {k: PRO_G + PIDX[k] + 1 for k in GCH}

    MSKt, EPIt, BFVt = {}, {}, {}
    cv = V0
    cv += 8
    PRO_V = cv
    cv += 1
    PRO_V2 = cv
    cv += 1
    PRO_V3 = cv
    for n, (hh, qm, kc, last_kc, dm) in enumerate(sched):
        gqm = hh * 16 + qm
        if dm is not None:
            cv += 1
            MSKt[n] = cv
        if last_kc:
            cv += 2
            EPIt[gqm] = cv
            for k in VCH:
                if slot_bfv[k] == gqm:
                    cv += 1
                    BFVt[k] = cv

    EXPPt, AEVq, AEV_LAST = {}, [], {}
    ca = A0 + 1
    for p in range(NPAIR):
        ca += 1
        EXPPt[p] = ca
        n1 = 2 * p + 1
        hh, qm, kc1, last1, _ = sched[n1]
        gqm = hh * 16 + qm
        if last1:
            for kind_, id_ in tp_by_slot.get(gqm, []):
                for j in range(16):
                    ca += 1
                    AEVq.append(ca)
                AEV_LAST[(kind_, id_)] = ca
    PREV_U = {}
    for lst in (VCH, GCH):
        for i, k in enumerate(lst):
            if i >= 2:
                PREV_U[k] = lst[i - 2]

    TPdone, SMMd, SUMMd, ATMMd = {}, {}, {}, {}
    TPtile = []
    cm = MM0 + 4
    _sc = [cm]

    def _s_count(m):
        _sc[0] += 1
        SMMd[m] = _sc[0]

    _s_count(0)
    _s_count(1)
    for n, (hh, qm, kc, last_kc, dm) in enumerate(sched):
        if n % 2 == 0:
            if n + 2 < NSC:
                _s_count(n + 2)
            if n + 3 < NSC:
                _s_count(n + 3)
        _sc[0] += 1
        _sc[0] += 1
        SUMMd[n] = _sc[0]
        if last_kc:
            gqm = hh * 16 + qm
            if qm == 15:
                ATMMd[hh] = _sc[0]
            for kind_, id_ in tp_by_slot.get(gqm, []):
                for _j in range(16):
                    _sc[0] += 1
                    TPtile.append(_sc[0])
                TPdone[(kind_, id_)] = _sc[0]

    with nc.Block() as blk, contextlib.ExitStack() as ph:
        qb = [ph.enter_context(nc.sbuf_tensor(f"qb{p}", [128, T], BF16)) for p in range(2)]
        kb = [ph.enter_context(nc.sbuf_tensor(f"kb{p}", [128, T], BF16)) for p in range(2)]
        vb = [ph.enter_context(nc.sbuf_tensor(f"vb{p}", [128, T], BF16)) for p in range(2)]
        pt = [ph.enter_context(nc.sbuf_tensor(f"pt{p}", [128, 1024], BF16)) for p in range(3)]
        osb = [ph.enter_context(nc.sbuf_tensor(f"osb{p}", [128, 512], BF16)) for p in range(2)]
        ssb = [ph.enter_context(nc.sbuf_tensor(f"ssb{p}", [1, 512], F32)) for p in range(2)]
        bfvA = ph.enter_context(nc.sbuf_tensor("bfvA", [128, D], F32))
        bfvB = ph.enter_context(nc.sbuf_tensor("bfvB", [128, D], F32))
        bfgA = ph.enter_context(nc.sbuf_tensor("bfgA", [128, D], F32))
        bfgB = ph.enter_context(nc.sbuf_tensor("bfgB", [128, D], F32))
        braw = [ph.enter_context(nc.sbuf_tensor(f"braw{p}", [128, D], F32)) for p in range(2)]
        tpe_v = ph.enter_context(nc.sbuf_tensor("tpe_v", [128, 2 * D], BF16))
        tpe_g = ph.enter_context(nc.sbuf_tensor("tpe_g", [128, 2 * D], BF16))
        tpe_d = ph.enter_context(nc.sbuf_tensor("tpe_d", [128, 2 * D], BF16))
        ps_s4 = ph.enter_context(nc.psum_tensor("ps_s4", [128, 2048], F32))
        ps_acc = ph.enter_context(nc.psum_tensor("ps_acc", [128, 512], F32))
        ps_sum = ph.enter_context(nc.psum_tensor("ps_sum", [128, 512], F32))
        ps_tp = [ph.enter_context(nc.psum_tensor(f"ps_tp{j}", [128, 512], F32)) for j in range(2)]
        TPE = {"v": tpe_v, "g": tpe_g}
        BFB = {"v": bfvB, "g": bfgB}

        @blk.gpsimd
        def _(g):
            g.wait_ge(sv.h, PRO_V)
            g.tensor_tensor(brow[:1, 8:12], brow[:1, 0:4], brow[:1, 4:8], AOP.mult).then_inc(g2.h, 1)
            assert g2.inc() == PRO_G

            def e2ag(hh):
                g.wait_ge(*stE2.tok(hh * 16 + 15))
                g.collective_compute("AllGather", AOP.bypass,
                                     replica_groups=[list(range(NCORES))],
                                     ins=[e2in_h[hh].ap().opt()],
                                     outs=[e2out_h[hh].ap().opt()]).then_inc(hcc[hh].h, 1)
                hcc[hh].inc()
                g.collective_compute("AllGather", AOP.bypass,
                                     replica_groups=[list(range(NCORES))],
                                     ins=[e2sin[hh:hh + 1, :].opt()],
                                     outs=[e2sout_h[hh].ap().opt()]).then_inc(hcc[hh].h, 1)
                hcc[hh].inc()

            for i, k in enumerate(GCH):
                g.wait_ge(*stTFg.tok(i))
                if i >= 1:
                    g.wait_ge(mm.h, TPdone[("u", GCH[i - 1])])
                sdb, ddb = bfgA, bfgB
                h = 1
                last = None
                while h < D:
                    nb = D // (2 * h)
                    a_ = bass.AP(sdb, 0, [[D, 128], [2 * h, nb], [1, h]])
                    b_ = bass.AP(sdb, h, [[D, 128], [2 * h, nb], [1, h]])
                    oa = bass.AP(ddb, 0, [[D, 128], [2 * h, nb], [1, h]])
                    ob = bass.AP(ddb, h, [[D, 128], [2 * h, nb], [1, h]])
                    g.tensor_tensor(oa, a_, b_, AOP.add)
                    last = g.tensor_tensor(ob, a_, b_, AOP.subtract)
                    sdb, ddb = ddb, sdb
                    h *= 2
                assert sdb is bfgB
                last.then_inc(g2.h, 1)
                assert g2.inc() == GBF[k]
            e2ag(0)
            g.wait_ge(*stWrU.tok(15))
            g.collective_compute("AllGather", AOP.bypass,
                                 replica_groups=[list(range(NCORES))],
                                 ins=[e3u_in.ap().opt()],
                                 outs=[e3u_out.ap().opt()]).then_inc(cc2.h, 1)
            cc2.inc()
            e2ag(1)
            e2ag(2)
            e2ag(3)
            g.wait_ge(*stWrD.tok(7))
            g.collective_compute("AllGather", AOP.bypass,
                                 replica_groups=[list(range(NCORES))],
                                 ins=[e3d_in.ap().opt()],
                                 outs=[e3d_out.ap().opt()]).then_inc(cc2.h, 1)
            cc2.inc()

        @blk.tensor
        def _(t):
            t.wait_ge(sv.h, PRO_V2)
            for h_ in range(4):
                t.matmul(ps_sum[:, h_:h_ + 1], ones_f1[:1, :], brow[:1, 8 + h_:9 + h_],
                         start=True, stop=True).then_inc(mm.h, 1)
                mm.inc()
            assert mm.n == PRO_MM
            qglob = [0]

            def emit_tp_groups(s_):
                for kind_, id_ in tp_by_slot.get(s_, []):
                    if kind_ == "u":
                        k = id_
                        pipe = PIPE[k]
                        if pipe == "v":
                            t.wait_ge(sv.h, BFVt[k])
                        else:
                            t.wait_ge(g2.h, GBF[k])
                        srcbuf = BFB[pipe]
                    else:
                        t.wait_ge(*stDL.tok(id_))
                        srcbuf = braw[id_ % 2]
                    for j in range(16):
                        q = qglob[0]
                        if q >= 2:
                            t.wait_ge(sa.h, AEVq[q - 2])
                        t.transpose(ps_tp[q % 2][:, 0:128], srcbuf[:, j * 128:(j + 1) * 128],
                                    ident_sb[:, :]).then_inc(mm.h, 1)
                        mm.inc()
                        qglob[0] += 1
                    assert TPdone[(kind_, id_)] == mm.n

            def emit_s(m):
                hh_m, qm_m, kc_m, _, _ = sched[m]
                if kc_m == 0 and qm_m == 0:
                    t.wait_ge(*stH.tok(hh_m))
                if m >= 4:
                    t.wait_ge(sa.h, EXPPt[m // 2 - 2])
                t.matmul(ps_s4[:, (m % 4) * 512:(m % 4 + 1) * 512],
                         kb[hh_m % 2][:, kc_m * 128:(kc_m + 1) * 128],
                         qb[hh_m % 2][:, qm_m * 512:(qm_m + 1) * 512],
                         start=True, stop=True).then_inc(mm.h, 1)
                assert SMMd[m] == mm.inc()

            emit_s(0)
            emit_s(1)
            for n, (hh, qm, kc, last_kc, dm) in enumerate(sched):
                gqm = hh * 16 + qm
                if n % 2 == 0:
                    if n + 2 < NSC:
                        emit_s(n + 2)
                    if n + 3 < NSC:
                        emit_s(n + 3)
                t.wait_ge(sa.h, EXPPt[n // 2])
                if kc == 0 and gqm >= 1:
                    t.wait_ge(sv.h, EPIt[gqm - 1])
                pr = n // 2
                half = (n % 2) * 512
                t.matmul(ps_acc[:, :], vb[hh % 2][:, kc * 128:(kc + 1) * 128],
                         pt[pr % 3][:, half:half + 512], start=(kc == 0), stop=last_kc).then_inc(mm.h, 1)
                mm.inc()
                t.matmul(ps_sum[:, :], ones128[:, :], pt[pr % 3][:, half:half + 512],
                         start=(kc == 0), stop=last_kc).then_inc(mm.h, 1)
                assert SUMMd[n] == mm.inc()
                if last_kc:
                    if qm == 15:
                        assert ATMMd[hh] == mm.n
                    emit_tp_groups(gqm)

        @blk.scalar
        def _(s):
            s.wait_ge(g2.h, PRO_G)
            s.sqrt(brow[:1, 12:16], brow[:1, 8:12]).then_inc(sa.h, 1)
            assert sa.inc() == PRO_A
            qglob = [0]
            wr_tok_u = {}
            wr_tok_d = {}

            def emit_ev_groups(s_):
                for kind_, id_ in tp_by_slot.get(s_, []):
                    if kind_ == "u":
                        pipe = PIPE[id_]
                        pidx = PIDX[id_]
                        tpe = TPE[pipe]
                        soff = (pidx % 2) * D
                        if pidx >= 2:
                            s.wait_ge(*stWrU.tok(PREV_U[id_]))
                    else:
                        tpe = tpe_d
                        soff = (id_ % 2) * D
                        if id_ >= 2:
                            s.wait_ge(*stWrD.tok(id_ - 2))
                    for j in range(16):
                        q = qglob[0]
                        s.wait_ge(mm.h, TPtile[q])
                        if kind_ == "u":
                            scl_ap = suu_sb[:, j:j + 1]
                        else:
                            dimblock = id_ // 2
                            scl_ap = sud_sb[:, dimblock * 16 + j: dimblock * 16 + j + 1]
                        s.activation(tpe[:, soff + j * 128: soff + (j + 1) * 128],
                                     ps_tp[q % 2][:, 0:128], AF.Copy, scale=scl_ap).then_inc(sa.h, 1)
                        got = sa.inc()
                        assert got == AEVq[q], (got, AEVq[q])
                        qglob[0] += 1

            for p in range(NPAIR):
                n0, n1 = 2 * p, 2 * p + 1
                hh, qm, kc1, last1, dm1 = sched[n1]
                _, _, kc0, _, dm0 = sched[n0]
                gqm = hh * 16 + qm
                if p == 0:
                    s.wait_ge(sv.h, PRO_V3)
                if dm1 is not None:
                    s.wait_ge(sv.h, MSKt[n1])
                else:
                    s.wait_ge(mm.h, SMMd[n1])
                if p >= 3:
                    s.wait_ge(mm.h, SUMMd[2 * (p - 3) + 1])
                s.activation(pt[p % 3][:, :], ps_s4[:, (n0 % 4) * 512:(n0 % 4) * 512 + 1024], AF.Exp,
                             bias=bias4[:, hh:hh + 1], scale=SCL).then_inc(sa.h, 1)
                assert sa.inc() == EXPPt[p]
                if last1:
                    emit_ev_groups(gqm)

        @blk.vector
        def _(v):
            for h_ in range(4):
                v.reduce_max(brow[:1, h_:h_ + 1], bquad[:1, h_ * 8:(h_ + 1) * 8], axis=AX.X).then_inc(sv.h, 1)
                sv.inc()
                v.reduce_max(brow[:1, 4 + h_:5 + h_], bquad[:1, 32 + h_ * 8: 32 + (h_ + 1) * 8],
                             axis=AX.X).then_inc(sv.h, 1)
                sv.inc()
            assert sv.n == PRO_V
            v.wait_ge(sa.h, PRO_A)
            v.tensor_scalar(brow[:1, 8:12], brow[:1, 12:16], -1.03 * SCL, -1.0,
                            AOP.mult, AOP.add).then_inc(sv.h, 1)
            assert sv.inc() == PRO_V2
            v.wait_ge(mm.h, PRO_MM)
            v.tensor_copy(bias4[:, 0:4], ps_sum[:, 0:4]).then_inc(sv.h, 1)
            assert sv.inc() == PRO_V3

            def emit_bfv(s_):
                for i, k in enumerate(VCH):
                    if slot_bfv[k] != s_:
                        continue
                    v.wait_ge(*stTFv.tok(i))
                    if i >= 1:
                        v.wait_ge(mm.h, TPdone[("u", VCH[i - 1])])
                    sdb, ddb = bfvA, bfvB
                    h = 1
                    last = None
                    while h < D:
                        nb = D // (2 * h)
                        a_ = bass.AP(sdb, 0, [[D, 128], [2 * h, nb], [1, h]])
                        b_ = bass.AP(sdb, h, [[D, 128], [2 * h, nb], [1, h]])
                        oa = bass.AP(ddb, 0, [[D, 128], [2 * h, nb], [1, h]])
                        ob = bass.AP(ddb, h, [[D, 128], [2 * h, nb], [1, h]])
                        v.tensor_tensor(oa, a_, b_, AOP.add)
                        last = v.tensor_tensor(ob, a_, b_, AOP.subtract)
                        sdb, ddb = ddb, sdb
                        h *= 2
                    assert sdb is bfvB
                    last.then_inc(sv.h, 1)
                    assert sv.inc() == BFVt[k]

            for n, (hh, qm, kc, last_kc, dm) in enumerate(sched):
                gqm = hh * 16 + qm
                if dm is not None:
                    v.wait_ge(mm.h, SMMd[n])
                    sl = (n % 4) * 512
                    v.tensor_tensor(ps_s4[:, sl:sl + 512], ps_s4[:, sl:sl + 512],
                                    mask_sb[:, 384 - 128 * dm: 896 - 128 * dm], AOP.add).then_inc(sv.h, 1)
                    assert sv.inc() == MSKt[n]
                if last_kc:
                    v.wait_ge(mm.h, SUMMd[n])
                    if gqm >= 2:
                        v.wait_ge(*stE2.tok(gqm - 2))
                    v.tensor_copy(osb[qm % 2][:, :], ps_acc[:, :]).then_inc(sv.h, 1)
                    sv.inc()
                    v.tensor_copy(ssb[qm % 2][:1, :], ps_sum[0:1, :]).then_inc(sv.h, 1)
                    assert sv.inc() == EPIt[gqm]
                    emit_bfv(gqm)

        @blk.sync
        def _(sy):
            sy.wait_ge(*stQs.tok(31))
            sy.wait_ge(*stKs.tok(31))
            sy.wait_ge(*stVs.tok(63))
            sy.wait_ge(*stRW.tok(1))

            def head_loads(hh):
                stH.issue(sy.dma_start(out=qb[hh % 2][:, :], in_=ap2(qs, hh * 128, 128, 0, T)))
                stH.issue(sy.dma_start(out=kb[hh % 2][:, :], in_=ap2(ks, hh * 128, 128, 0, T)))
                stH.issue(sy.dma_start(out=vb[hh % 2][:, :], in_=ap3(vs[hh], 0, 0, 128, 128 * 128, 64, 128)))

            def tf_loads(s_):
                for k in range(16):
                    if slot_ld[k] != s_:
                        continue
                    kind = "u" if k < 8 else "g"
                    c = k % 8
                    srcw = wup_r if kind == "u" else wgt_r
                    pipe = PIPE[k]
                    pidx = PIDX[k]
                    stf = stTFv if pipe == "v" else stTFg
                    dstb = bfvA if pipe == "v" else bfgA
                    if pidx >= 1:
                        prevk = (VCH if pipe == "v" else GCH)[pidx - 1]
                        if pipe == "v":
                            sy.wait_ge(sv.h, BFVt[prevk])
                        else:
                            sy.wait_ge(g2.h, GBF[prevk])
                    stf.issue(sy.dma_start(out=dstb[:, :], in_=ap2(srcw, c * 128, 128, 0, D)))
                for g_ in range(8):
                    if slot_ld.get(("d", g_)) != s_:
                        continue
                    if g_ >= 2:
                        sy.wait_ge(mm.h, TPdone[("d", g_ - 2)])
                    rc, dimblock = g_ % 2, g_ // 2
                    stDL.issue(sy.dma_start(out=braw[g_ % 2][:, :],
                                            in_=ap2(raw_d, rc * 128, 128, dimblock * D, D)))

            def tf_writes(s_):
                for k in range(16):
                    if slot_wr[k] != s_:
                        continue
                    pipe = PIPE[k]
                    pidx = PIDX[k]
                    tpe = TPE[pipe]
                    soff = (pidx % 2) * D
                    sy.wait_ge(sa.h, AEV_LAST[("u", k)])
                    col0 = (0 if k < 8 else 1024) + (k % 8) * 128
                    stWrU.issue(sy.dma_start(
                        out=ap3(e3u_in, 0, col0, 128, 128 * 2048, 16, 128),
                        in_=sb3(tpe, soff, 128, 16, 128)))
                for g_ in range(8):
                    if slot_wr_d[g_] != s_:
                        continue
                    sy.wait_ge(sa.h, AEV_LAST[("d", g_)])
                    rc, dimblock = g_ % 2, g_ // 2
                    soff = (g_ % 2) * D
                    stWrD.issue(sy.dma_start(
                        out=ap3(e3d_in, dimblock * 2048, rc * 128, 128, 128 * 256, 16, 128),
                        in_=sb3(tpe_d, soff, 128, 16, 128)))

            tf_loads(0)
            head_loads(0)
            for hh in range(HPC):
                for qm in range(16):
                    gqm = hh * 16 + qm
                    if qm == 2 and hh + 1 < HPC:
                        if hh >= 1:
                            sy.wait_ge(mm.h, ATMMd[hh - 1])
                        head_loads(hh + 1)
                    if gqm >= 1:
                        tf_loads(gqm)
                    sy.wait_ge(sv.h, EPIt[gqm])
                    stE2.issue(sy.dma_start(out=ap2(e2in_h[hh], 0, 128, qm * 512, 512), in_=osb[qm % 2][:, :]))
                    stE2.issue(sy.dma_start(out=ap2(e2sin, hh, 1, qm * 512, 512), in_=ssb[qm % 2][:1, :]))
                    tf_writes(gqm)

    E2_CC = cc.n
    E3_CC = cc2.n
    if stages <= 3:
        return _finish(3)

    V0, MM0, A0 = sv.n, mm.n, sa.n
    stSum = mk([GP[2]])
    stAtn = mk([GP[3], GP[4]])
    stWo = mk([GP[5], GP[0]])
    stX = mk([SY[0], SY[1]])
    stX1 = mk([SY[2], SY[3]])
    SUMI = V0 + 1
    RCPh = lambda h: V0 + 2 + 5 * h
    ATN = lambda i: V0 + 3 + 5 * (i // 4) + (i % 4)
    X1V = lambda o: V0 + 21 + 3 * o + 1
    XAC = lambda o: V0 + 21 + 3 * o + 2
    X2A = lambda o: V0 + 21 + 3 * o + 3
    V_C2 = V0 + 21 + 48 + 1
    RECIP2 = V0 + 21 + 48 + 2
    SQ5 = lambda o: A0 + o + 1
    SQT2 = A0 + 17
    MMRB = lambda i: MM0 + i + 1
    WOZ = lambda o: (MM0 + 17 + o) if o < 2 else (MM0 + 3 * o + 15)
    XSM = lambda o: (MM0 + 19 + 3 * o) if o < 15 else (MM0 + 63)
    X2M = lambda o: (MM0 + 20 + 3 * o) if o < 15 else (MM0 + 64)

    ln2_rows = contextlib.ExitStack()
    s_r2 = ln2_rows.enter_context(nc.sbuf_tensor("ln2_sr", [1, TC], F32))
    xsum_sb = ln2_rows.enter_context(nc.sbuf_tensor("xsum_sb", [1, TC], F32))
    x2sum_sb = ln2_rows.enter_context(nc.sbuf_tensor("x2sum_sb", [1, TC], F32))

    with nc.Block() as blk, contextlib.ExitStack() as ph:
        sums16 = ph.enter_context(nc.sbuf_tensor("sums16", [16, TC], F32))
        recip16 = ph.enter_context(nc.sbuf_tensor("recip16", [16, TC], F32))
        tmpa = ph.enter_context(nc.sbuf_tensor("tmpa", [128, 4 * TC], BF16))
        attn_sb = ph.enter_context(nc.sbuf_tensor("attn_sb", [128, 16 * TC], BF16))
        wo_sb = [ph.enter_context(nc.sbuf_tensor(f"wo_sb{p}", [128, 16 * 128], BF16)) for p in range(2)]
        xcb = [ph.enter_context(nc.sbuf_tensor(f"xcb{p}", [128, TC], F32)) for p in range(3)]
        x1b = [ph.enter_context(nc.sbuf_tensor(f"x1b{p}", [128, TC], F32)) for p in range(3)]
        sq5 = [ph.enter_context(nc.sbuf_tensor(f"sq5_{p}", [128, TC], F32)) for p in range(2)]
        ps_misc = ph.enter_context(nc.psum_tensor("ps_misc", [128, TC], F32))
        ps_z = ph.enter_context(nc.psum_tensor("ps_z", [128, TC], F32))

        @blk.gpsimd
        def _(g):
            with g.register("r1") as r1, g.register("r2") as r2:
                g.reg_load(r2, tok_sb[:1, 3:4])
                g.reg_load(r1, tok_sb[:1, 2:3])
                for hh in range(4):
                    g.wait_ge(hcc[hh].h, 2)
                    if hh == 0:
                        g.wait_ge(sv.h, SUMI)
                    stSum.issue(g.dma_start(out=sums16[4 * hh:4 * hh + 4, :],
                                            in_=bass.AP(e2sout_h[hh], r2, [[T, 4], [1, TC]])))
                    if hh >= 1:
                        g.wait_ge(sv.h, ATN(4 * (hh - 1) + 3))
                    stAtn.issue(g.dma_start(
                        out=tmpa[:, :],
                        in_=bass.AP(e2out_h[hh], r1, [[T, 128], [128 * T, 4], [1, TC]])))
            for o in range(16):
                if o >= 2:
                    g.wait_ge(mm.h, WOZ(o - 2))
                stWo.issue(g.dma_start(out=wo_sb[o % 2][:, :], in_=ap3(woT, 0, o * 128, 128, 128 * D, 16, 128)))

        @blk.tensor
        def _(t):
            for j in range(16):
                t.wait_ge(sv.h, RCPh(j // 4))
                if j >= 1:
                    t.wait_ge(sv.h, ATN(j - 1))
                last = None
                for c4 in range(4):
                    last = t.matmul(ps_misc[:, c4 * 512:(c4 + 1) * 512],
                                    sel_sb[:, j * 128:(j + 1) * 128],
                                    recip16[:, c4 * 512:(c4 + 1) * 512], start=True, stop=True)
                last.then_inc(mm.h, 1)
                assert mm.inc() == MMRB(j)
            for o in range(18):
                if o >= 2:
                    op = o - 2
                    t.wait_ge(sv.h, X1V(op))
                    if op >= 1:
                        t.wait_ge(sv.h, X2A(op - 1))
                    last = None
                    for c4 in range(4):
                        last = t.matmul(ps_misc[0:1, c4 * 512:(c4 + 1) * 512], ones_fc[:, :1],
                                        x1b[op % 3][:, c4 * 512:(c4 + 1) * 512], start=True, stop=True)
                    last.then_inc(mm.h, 1)
                    assert mm.inc() == XSM(op)
                    t.wait_ge(sa.h, SQ5(op))
                    t.wait_ge(sv.h, XAC(op))
                    last = None
                    for c4 in range(4):
                        last = t.matmul(ps_misc[0:1, c4 * 512:(c4 + 1) * 512], ones_fc[:, :1],
                                        sq5[op % 2][:, c4 * 512:(c4 + 1) * 512], start=True, stop=True)
                    last.then_inc(mm.h, 1)
                    assert mm.inc() == X2M(op)
                if o < 16:
                    t.wait_ge(*stWo.tok(o))
                    if o == 0:
                        t.wait_ge(sv.h, ATN(15))
                    if o >= 1:
                        t.wait_ge(sv.h, X1V(o - 1))
                    last = None
                    for j in range(16):
                        hmod = 4 * (j % 4) + j // 4
                        for c4 in range(4):
                            last = t.matmul(ps_z[:, c4 * 512:(c4 + 1) * 512],
                                            wo_sb[o % 2][:, hmod * 128:(hmod + 1) * 128],
                                            attn_sb[:, j * TC + c4 * 512: j * TC + (c4 + 1) * 512],
                                            start=(j == 0), stop=(j == 15))
                    last.then_inc(mm.h, 1)
                    assert mm.inc() == WOZ(o)

        @blk.scalar
        def _(s):
            for o in range(16):
                s.wait_ge(sv.h, X1V(o))
                if o >= 2:
                    s.wait_ge(mm.h, X2M(o - 2))
                s.square(sq5[o % 2][:, :], x1b[o % 3][:, :]).then_inc(sa.h, 1)
                assert sa.inc() == SQ5(o)
            s.wait_ge(sv.h, V_C2)
            s.activation(x2sum_sb[:1, :], x2sum_sb[:1, :], AF.Sqrt, bias=eps_sb[:1, :1]).then_inc(sa.h, 1)
            assert sa.inc() == SQT2

        @blk.vector
        def _(v):
            v.memset(sums16[:, :], 1.0).then_inc(sv.h, 1)
            assert sv.inc() == SUMI
            for hh in range(4):
                v.wait_ge(*stSum.tok(hh))
                v.tensor_scalar(recip16[:, :], sums16[:, :], 1e-30, None, AOP.add)
                v.reciprocal(recip16[:, :], recip16[:, :]).then_inc(sv.h, 1)
                assert sv.inc() == RCPh(hh)
                for j in range(4 * hh, 4 * hh + 4):
                    rk = j % 4
                    v.wait_ge(*stAtn.tok(hh))
                    v.wait_ge(mm.h, MMRB(j))
                    v.tensor_tensor(attn_sb[:, j * TC:(j + 1) * TC], tmpa[:, rk * TC:(rk + 1) * TC],
                                    ps_misc[:, :], AOP.mult).then_inc(sv.h, 1)
                    assert sv.inc() == ATN(j)
            for o in range(16):
                v.wait_ge(mm.h, WOZ(o))
                v.wait_ge(*stX.tok(o))
                if o >= 3:
                    v.wait_ge(*stX1.tok(o - 3))
                    v.wait_ge(mm.h, X2M(o - 3))
                v.tensor_tensor(x1b[o % 3][:, :], ps_z[:, :], xcb[o % 3][:, :], AOP.add).then_inc(sv.h, 1)
                assert sv.inc() == X1V(o)
                v.wait_ge(mm.h, XSM(o))
                if o == 0:
                    v.tensor_copy(xsum_sb[:1, :], ps_misc[0:1, :]).then_inc(sv.h, 1)
                else:
                    v.tensor_tensor(xsum_sb[:1, :], xsum_sb[:1, :], ps_misc[0:1, :], AOP.add).then_inc(sv.h, 1)
                assert sv.inc() == XAC(o)
                v.wait_ge(mm.h, X2M(o))
                if o == 0:
                    v.tensor_copy(x2sum_sb[:1, :], ps_misc[0:1, :]).then_inc(sv.h, 1)
                else:
                    v.tensor_tensor(x2sum_sb[:1, :], x2sum_sb[:1, :], ps_misc[0:1, :], AOP.add).then_inc(sv.h, 1)
                assert sv.inc() == X2A(o)
            v.tensor_scalar(xsum_sb[:1, :], xsum_sb[:1, :], 1.0 / D, None, AOP.mult)
            v.tensor_scalar(x2sum_sb[:1, :], x2sum_sb[:1, :], 1.0 / D, None, AOP.mult)
            v.tensor_tensor(s_r2[:1, :], xsum_sb[:1, :], xsum_sb[:1, :], AOP.mult)
            v.tensor_tensor(x2sum_sb[:1, :], x2sum_sb[:1, :], s_r2[:1, :], AOP.subtract).then_inc(sv.h, 1)
            assert sv.inc() == V_C2
            v.wait_ge(sa.h, SQT2)
            v.reciprocal(s_r2[:1, :], x2sum_sb[:1, :]).then_inc(sv.h, 1)
            assert sv.inc() == RECIP2

        @blk.sync
        def _(sy):
            for o in range(18):
                if o < 16:
                    if o >= 3:
                        sy.wait_ge(sv.h, X1V(o - 3))
                    stX.issue(sy.dma_start(out=xcb[o % 3][:, :], in_=ap2(x_fm, o * 128, 128, 0, TC)))
                if o >= 2:
                    sy.wait_ge(sv.h, X1V(o - 2))
                    stX1.issue(sy.dma_start(out=ap2(x1sp, (o - 2) * 128, 128, 0, TC),
                                            in_=x1b[(o - 2) % 3][:, :]))

    if stages <= 4:
        return _finish(4)

    V0, MM0 = sv.n, mm.n
    stL2 = mk([SY[4], SY[5]])
    stS2 = mk([SY[7], SY[8]])
    BC2 = MM0 + 1
    NRM2 = [V0 + i + 1 for i in range(16)]

    with nc.Block() as blk, contextlib.ExitStack() as ph:
        xc2 = [ph.enter_context(nc.sbuf_tensor(f"l2_xc{p}", [128, TC], F32)) for p in range(2)]
        xn2 = [ph.enter_context(nc.sbuf_tensor(f"l2_xn{p}", [128, TC], BF16)) for p in range(2)]
        t12 = ph.enter_context(nc.sbuf_tensor("l2_t1", [128, TC], F32))
        psm2 = ph.enter_context(nc.psum_tensor("l2_psm", [128, TC], F32))
        psr2 = ph.enter_context(nc.psum_tensor("l2_psr", [128, TC], F32))

        @blk.sync
        def _(sy):
            sy.wait_ge(*stX1.tok(15))
            for i in range(18):
                if i < 16:
                    if i >= 2:
                        sy.wait_ge(sv.h, NRM2[i - 2])
                    stL2.issue(sy.dma_start(out=xc2[i % 2][:, :], in_=ap2(x1sp, i * 128, 128, 0, TC)))
                if i >= 2:
                    sy.wait_ge(sv.h, NRM2[i - 2])
                    stS2.issue(sy.dma_start(out=ap2(u_dram, (i - 2) * 128, 128, 0, TC),
                                            in_=xn2[(i - 2) % 2][:, :]))

        @blk.tensor
        def _(t):
            t.wait_ge(sv.h, RECIP2)
            last = None
            for c4 in range(4):
                t.matmul(psm2[:, c4 * 512:(c4 + 1) * 512], ones_f1[:1, :],
                         xsum_sb[:1, c4 * 512:(c4 + 1) * 512], start=True, stop=True)
                last = t.matmul(psr2[:, c4 * 512:(c4 + 1) * 512], ones_f1[:1, :],
                                s_r2[:1, c4 * 512:(c4 + 1) * 512], start=True, stop=True)
            last.then_inc(mm.h, 1)
            assert mm.inc() == BC2

        @blk.vector
        def _(v):
            for i in range(16):
                v.wait_ge(*stL2.tok(i))
                v.wait_ge(mm.h, BC2)
                if i >= 2:
                    v.wait_ge(*stS2.tok(i - 2))
                v.tensor_tensor(t12[:, :], xc2[i % 2][:, :], psm2[:, :], AOP.subtract)
                v.tensor_tensor(xn2[i % 2][:, :], t12[:, :], psr2[:, :], AOP.mult).then_inc(sv.h, 1)
                assert sv.inc() == NRM2[i]

    ln2_rows.close()
    ln2_stS = stS2
    if stages <= 5:
        return _finish(5)

    V0, MM0, A0 = sv.n, mm.n, sa.n
    NJ = I8 // 128
    NR = D // 128
    stU = mk([SY[9]])
    stUG = mk([SY[10], SY[11]], gsize=2)
    stDW = mk([SY[6], SY[0]])
    stXc = mk([SY[1], SY[2]])
    stO = mk([SY[3], SY[4]])
    UMM = lambda tb, j: MM0 + 144 * tb + 2 * j + 1
    GMM = lambda tb, j: MM0 + 144 * tb + 2 * j + 2
    DMM = lambda tb, r: MM0 + 144 * tb + 128 + r + 1
    UEV = lambda tb, j: A0 + 128 * tb + 2 * j + 1
    GEV = lambda tb, j: A0 + 128 * tb + 2 * j + 2
    GPR = lambda tb, j: V0 + 80 * tb + j + 1
    OUTV = lambda tb, r: V0 + 80 * tb + 64 + r + 1

    def prev2(tb, k, nk):
        return (tb, k - 2) if k >= 2 else (tb - 1, nk - 2 + k)

    with nc.Block() as blk, contextlib.ExitStack() as ph:
        utb = ph.enter_context(nc.sbuf_tensor("utb", [128, 16 * TBLK], BF16))
        g_sb = ph.enter_context(nc.sbuf_tensor("g_sb", [128, 64 * TBLK], BF16))
        upt = [ph.enter_context(nc.sbuf_tensor(f"upt{p}", [128, 16 * 128], BF16)) for p in range(2)]
        gat = [ph.enter_context(nc.sbuf_tensor(f"gat{p}", [128, 16 * 128], BF16)) for p in range(2)]
        dwt = [ph.enter_context(nc.sbuf_tensor(f"dwt{p}", [128, 64 * 128], BF16)) for p in range(2)]
        tup = [ph.enter_context(nc.sbuf_tensor(f"tup{p}", [128, TBLK], F32)) for p in range(2)]
        tgt = [ph.enter_context(nc.sbuf_tensor(f"tgt{p}", [128, TBLK], F32)) for p in range(2)]
        x1c = [ph.enter_context(nc.sbuf_tensor(f"x1c{p}", [128, TBLK], F32)) for p in range(2)]
        outb = [ph.enter_context(nc.sbuf_tensor(f"outb{p}", [128, TBLK], F32)) for p in range(2)]
        ps_up = [ph.enter_context(nc.psum_tensor(f"ps_up{p}", [128, TBLK], F32)) for p in range(2)]
        ps_gt = [ph.enter_context(nc.psum_tensor(f"ps_gt{p}", [128, TBLK], F32)) for p in range(2)]
        ps_dn = [ph.enter_context(nc.psum_tensor(f"ps_dn{p}", [128, TBLK], F32)) for p in range(2)]

        @blk.sync
        def _(sy):
            sy.wait_ge(*ln2_stS.tok(15))
            sy.wait_ge(cc2.h, E3_CC)
            for tb in range(NTB):
                if tb >= 1:
                    sy.wait_ge(mm.h, GMM(tb - 1, NJ - 1))
                stU.issue(sy.dma_start(out=utb[:, :], in_=ap3(u_dram, 0, tb * TBLK, 128, 128 * TC, 16, TBLK)))
                for j in range(NJ):
                    if tb > 0 or j >= 2:
                        ptb, pj = prev2(tb, j, NJ)
                        sy.wait_ge(mm.h, GMM(ptb, pj))
                    rk, jl0 = j // 8, (j % 8) * 128
                    stUG.issue(sy.dma_start(out=upt[j % 2][:, :],
                                            in_=ap3(e3u_out, rk * D, jl0, 128, 128 * 2048, 16, 128)))
                    stUG.issue(sy.dma_start(out=gat[j % 2][:, :],
                                            in_=ap3(e3u_out, rk * D, 1024 + jl0, 128, 128 * 2048, 16, 128)))
                for r in range(NR + 2):
                    if r < NR:
                        if tb > 0 or r >= 2:
                            ptb, pr = prev2(tb, r, NR)
                            sy.wait_ge(mm.h, DMM(ptb, pr))
                        rkr, rl0 = r // 2, (r % 2) * 128
                        stDW.issue(sy.dma_start(out=dwt[r % 2][:, :],
                                                in_=ap3(e3d_out, rkr * I8, rl0, 128, 128 * 256, 64, 128)))
                        if tb > 0 or r >= 2:
                            ptb, pr = prev2(tb, r, NR)
                            sy.wait_ge(sv.h, OUTV(ptb, pr))
                        stXc.issue(sy.dma_start(out=x1c[r % 2][:, :],
                                                in_=ap2(x1sp, r * 128, 128, tb * TBLK, TBLK)))
                    if r >= 2:
                        sy.wait_ge(sv.h, OUTV(tb, r - 2))
                        stO.issue(sy.dma_start(out=ap2(out_fm, (r - 2) * 128, 128, tb * TBLK, TBLK),
                                               in_=outb[(r - 2) % 2][:, :]))
            sy.wait_ge(*stO.tok(NTB * NR - 1))

        @blk.tensor
        def _(t):
            t.wait_ge(cc2.h, E3_CC)
            for tb in range(NTB):
                t.wait_ge(*stU.tok(tb))
                for j in range(NJ):
                    t.wait_ge(*stUG.tok(64 * tb + j))
                    if tb > 0 or j >= 2:
                        ptb, pj = prev2(tb, j, NJ)
                        t.wait_ge(sa.h, UEV(ptb, pj))
                    last = None
                    for i in range(16):
                        last = t.matmul(ps_up[j % 2][:, :], upt[j % 2][:, i * 128:(i + 1) * 128],
                                        utb[:, i * TBLK:(i + 1) * TBLK], start=(i == 0), stop=(i == 15))
                    last.then_inc(mm.h, 1)
                    assert mm.inc() == UMM(tb, j)
                    if tb > 0 or j >= 2:
                        ptb, pj = prev2(tb, j, NJ)
                        t.wait_ge(sa.h, GEV(ptb, pj))
                    last = None
                    for i in range(16):
                        last = t.matmul(ps_gt[j % 2][:, :], gat[j % 2][:, i * 128:(i + 1) * 128],
                                        utb[:, i * TBLK:(i + 1) * TBLK], start=(i == 0), stop=(i == 15))
                    last.then_inc(mm.h, 1)
                    assert mm.inc() == GMM(tb, j)
                for r in range(NR):
                    t.wait_ge(*stDW.tok(16 * tb + r))
                    t.wait_ge(sv.h, GPR(tb, NJ - 1))
                    if tb > 0 or r >= 2:
                        ptb, pr = prev2(tb, r, NR)
                        t.wait_ge(sv.h, OUTV(ptb, pr))
                    last = None
                    for jc in range(NJ):
                        last = t.matmul(ps_dn[r % 2][:, :], dwt[r % 2][:, jc * 128:(jc + 1) * 128],
                                        g_sb[:, jc * TBLK:(jc + 1) * TBLK],
                                        start=(jc == 0), stop=(jc == NJ - 1))
                    last.then_inc(mm.h, 1)
                    assert mm.inc() == DMM(tb, r)

        @blk.scalar
        def _(s):
            for tb in range(NTB):
                for j in range(NJ):
                    s.wait_ge(mm.h, UMM(tb, j))
                    if tb > 0 or j >= 2:
                        ptb, pj = prev2(tb, j, NJ)
                        s.wait_ge(sv.h, GPR(ptb, pj))
                    s.activation(tup[j % 2][:, :], ps_up[j % 2][:, :], AF.Copy,
                                 scale=wsu_sb[:, j:j + 1]).then_inc(sa.h, 1)
                    assert sa.inc() == UEV(tb, j)
                    s.wait_ge(mm.h, GMM(tb, j))
                    s.activation(tgt[j % 2][:, :], ps_gt[j % 2][:, :], AF.Silu,
                                 scale=wsg_sb[:, j:j + 1]).then_inc(sa.h, 1)
                    assert sa.inc() == GEV(tb, j)

        @blk.vector
        def _(v):
            v.wait_ge(sv.h, WSC_V)
            for tb in range(NTB):
                for j in range(NJ):
                    v.wait_ge(sa.h, GEV(tb, j))
                    if tb >= 1:
                        v.wait_ge(mm.h, DMM(tb - 1, NR - 1))
                    v.tensor_tensor(g_sb[:, j * TBLK:(j + 1) * TBLK], tup[j % 2][:, :],
                                    tgt[j % 2][:, :], AOP.mult).then_inc(sv.h, 1)
                    assert sv.inc() == GPR(tb, j)
                for r in range(NR):
                    v.wait_ge(mm.h, DMM(tb, r))
                    v.wait_ge(*stXc.tok(16 * tb + r))
                    if tb > 0 or r >= 2:
                        ptb, pr = prev2(tb, r, NR)
                        v.wait_ge(*stO.tok(16 * ptb + pr))
                    v.scalar_tensor_tensor(outb[r % 2][:, :], ps_dn[r % 2][:, :],
                                           wsd_sb[:, r:r + 1], x1c[r % 2][:, :],
                                           AOP.mult, AOP.add).then_inc(sv.h, 1)
                    assert sv.inc() == OUTV(tb, r)

    return _finish(7)


def _host_constants():
    identm = np.eye(128, dtype=np.float32)
    kk = np.arange(128)[:, None]
    u = np.arange(896)[None, :]
    maskm = np.where(kk <= (u - 384), 0.0, -1.0e30).astype(np.float32)
    sel = np.zeros((16, 16 * 128), dtype=np.float32)
    for h in range(16):
        sel[h, h * 128:(h + 1) * 128] = 1.0
    return identm, maskm, sel


def kernel(L0_attn_x, L0_attn_Wq, L0_attn_Wk, L0_attn_Wv, L0_attn_Wo,
           L0_ffn_SU_ug, L0_ffn_SU_dp, L0_ffn_Wscale_ug, L0_ffn_Wscale_dp,
           L0_ffn_W_up, L0_ffn_W_gate, L0_ffn_W_down):
    x = np.asarray(L0_attn_x)
    Wq = np.asarray(L0_attn_Wq); Wk = np.asarray(L0_attn_Wk)
    Wv = np.asarray(L0_attn_Wv); Wo = np.asarray(L0_attn_Wo)
    SUu = np.asarray(L0_ffn_SU_ug); SUd = np.asarray(L0_ffn_SU_dp)
    Wsu = np.asarray(L0_ffn_Wscale_ug); Wsd = np.asarray(L0_ffn_Wscale_dp)
    Wup = np.asarray(L0_ffn_W_up); Wgt = np.asarray(L0_ffn_W_gate)
    Wdn = np.asarray(L0_ffn_W_down)

    identm, maskm, sel = _host_constants()
    WqT = np.ascontiguousarray(Wq.T); WkT = np.ascontiguousarray(Wk.T)
    WvT = np.ascontiguousarray(Wv.T); WoT = np.ascontiguousarray(Wo.T)
    su_ug2 = np.ascontiguousarray(SUu.reshape(16, 128).T)
    su_dp2 = np.ascontiguousarray(SUd.reshape(64, 128).T)
    wsc_up2 = np.ascontiguousarray(Wsu[:I8].reshape(64, 128).T)
    wsc_gt2 = np.ascontiguousarray(Wsu[I8:].reshape(64, 128).T)
    wsc_dp2 = np.ascontiguousarray(Wsd.reshape(16, 128).T)

    in_maps = []
    for c in range(NCORES):
        b, g = c // GS, c % GS
        hs = g * HPC * DH
        im = {
            "x_fm": np.ascontiguousarray(x[b, g * TC:(g + 1) * TC, :].T),
            "wqT": np.ascontiguousarray(WqT[:, hs:hs + HPC * DH]),
            "wkT": np.ascontiguousarray(WkT[:, hs:hs + HPC * DH]),
            "wvT": np.ascontiguousarray(WvT[:, hs:hs + HPC * DH]),
            "woT": WoT,
            "wup_r": np.ascontiguousarray(Wup[c * 1024:(c + 1) * 1024, :]),
            "wgt_r": np.ascontiguousarray(Wgt[c * 1024:(c + 1) * 1024, :]),
            "wdn_r": np.ascontiguousarray(Wdn[c * 256:(c + 1) * 256, :]),
            "su_ug": su_ug2, "su_dp": su_dp2,
            "wsc_up": wsc_up2, "wsc_gt": wsc_gt2, "wsc_dp": wsc_dp2,
            "ident": identm, "maskm": maskm, "sel16": sel,
            "tok0": np.array([[g * TC, b * GS * D * (TC // 2),
                               b * GS * DH * T + g * TC,
                               b * GS * T + g * TC]], dtype=np.int32),
        }
        in_maps.append(im)

    nc = build(debug=DEBUG, stages=STAGES)
    res = run_bass_kernel_spmd(nc, in_maps, core_ids=list(range(NCORES)))

    out = np.empty((B, T, D), dtype=np.float32)
    for c in range(NCORES):
        b, g = c // GS, c % GS
        out[b, g * TC:(g + 1) * TC, :] = res.results[c]["out_fm"].T
    if DEBUG:
        kernel._dbg = res.results
    return out
